# version 18
# baseline (speedup 1.0000x reference)
import contextlib
import numpy as np

import concourse.bass as bass
import concourse.mybir as mybir
from concourse.bass_utils import run_bass_kernel_spmd

F32 = mybir.dt.float32
BF16 = mybir.dt.bfloat16
AF = mybir.ActivationFunctionType
ALU = mybir.AluOpType
AX = mybir.AxisListType

D = 1024
NHEAD = 16
NGRP = 4
HD = 64
NTOK = 2048
NT = 16
NKT = 32
DFF = 4096
NEXP = 8
QB = 256
NQB = NTOK // QB
PAIRS = [(0, 4), (1, 5), (2, 6), (3, 7), (8, 12), (9, 13), (10, 14), (11, 15)]
PAIR_OF = {}
for _pr, _hs in enumerate(PAIRS):
    PAIR_OF[_hs[0]] = (_pr, 0)
    PAIR_OF[_hs[1]] = (_pr, 1)

ENGS = ("tensor", "vector", "scalar", "gpsimd", "sync")
DEBUG = False
LAST_PLAN = {}
HARDEN = (0, 0)


class Planner:

    NLANE = 6

    def __init__(self):
        self.ops = []
        self.nseq = {e: 0 for e in ENGS}
        self.ndma = {e: 0 for e in ENGS}
        self.last = {e: None for e in ENGS}

    def add(self, eng, fn, deps=(), dma=False, tag="", aps=False):
        if HARDEN[1] > HARDEN[0] and HARDEN[0] <= len(self.ops) < HARDEN[1] and self.ops:
            deps = list(deps) + [self.ops[-1]]
        flat = []
        for d in deps:
            if d is None:
                continue
            if isinstance(d, (list, tuple)):
                flat.extend(x for x in d if x is not None)
            else:
                flat.append(d)
        rec = {"eng": eng, "fn": fn, "deps": flat, "dma": dma,
               "seq": self.nseq[eng], "need": False, "tag": tag,
               "dmaseq": None, "aps": aps}
        self.nseq[eng] += 1
        if dma:
            lane = self.ndma[eng] % self.NLANE
            self.ndma[eng] += 1
            rec["lane"] = lane
            rec["laneuse"] = (self.ndma[eng] - 1) // self.NLANE + 1
            rec["dmaseq"] = self.ndma[eng]
        self.ops.append(rec)
        best = {}
        red = []
        for d in flat:
            if d["dma"]:
                red.append(d)
            elif d["eng"] != eng:
                k = d["eng"]
                if k not in best or d["seq"] > best[k]["seq"]:
                    best[k] = d
        for d in best.values():
            d["need"] = True
            red.append(d)
        rec["rdeps"] = red
        self.last[eng] = rec
        return rec

    def barrier_tokens(self):
        return [t for t in self.last.values() if t is not None]

    def prepare(self):
        self.inc_id = {}
        cnt = {e: 0 for e in ENGS}
        for rec in self.ops:
            if rec["need"]:
                cnt[rec["eng"]] += 1
            self.inc_id[id(rec)] = cnt[rec["eng"]]

    def emit_engine(self, ename, eng, prog_sem, dma_sem):
        DRAIN_DIST = 2
        APS_DIST = 8
        last_wait = {}
        drained_at = -1
        for rec in self.ops:
            if rec["eng"] != ename:
                continue
            need_drain = False
            lim = APS_DIST if rec["aps"] else DRAIN_DIST
            for d in rec["deps"]:
                if not d["dma"] and d["eng"] == ename:
                    if (d["seq"] > drained_at
                            and rec["seq"] - d["seq"] <= lim):
                        need_drain = True
            for d in rec["rdeps"]:
                if d["dma"]:
                    sem, val = dma_sem[d["eng"]][d["lane"]], 16 * d["laneuse"]
                else:
                    sem, val = prog_sem[d["eng"]], self.inc_id[id(d)]
                if last_wait.get(id(sem), -1) >= val:
                    continue
                eng.wait_ge(sem, val)
                last_wait[id(sem)] = val
            if need_drain:
                eng.drain()
                drained_at = rec["seq"] - 1
            if rec["dma"]:
                lsem = dma_sem[ename][rec["lane"]]
                lval = 16 * (rec["laneuse"] - 1)
                if lval > 0 and last_wait.get(id(lsem), -1) < lval:
                    eng.wait_ge(lsem, lval)
                    last_wait[id(lsem)] = lval
                inst = rec["fn"](eng)
                inst.then_inc(lsem, 16)
            else:
                inst = rec["fn"](eng)
                if rec["need"]:
                    inst.then_inc(prog_sem[ename], 1)


def build_program():
    nc = bass.Bass("TRN2", target_bir_lowering=False, detect_race_conditions=False)
    P = Planner()
    add = P.add

    xo = nc.dram_tensor("x_own", [NTOK, D], F32, kind="ExternalInput").ap()
    xr = nc.dram_tensor("x_oth", [NTOK, D], F32, kind="ExternalInput").ap()
    wq = nc.dram_tensor("wq", [D, D], F32, kind="ExternalInput").ap()
    bq = nc.dram_tensor("bq", [D], F32, kind="ExternalInput").ap()
    wk = nc.dram_tensor("wk", [D, 256], F32, kind="ExternalInput").ap()
    bk = nc.dram_tensor("bk", [256], F32, kind="ExternalInput").ap()
    wv = nc.dram_tensor("wv", [D, 256], F32, kind="ExternalInput").ap()
    bv = nc.dram_tensor("bv", [256], F32, kind="ExternalInput").ap()
    wo = nc.dram_tensor("wo", [D, D], F32, kind="ExternalInput").ap()
    bo = nc.dram_tensor("bo", [D], F32, kind="ExternalInput").ap()
    gw = nc.dram_tensor("gw", [D, NEXP], F32, kind="ExternalInput").ap()
    gb = nc.dram_tensor("gb", [NEXP], F32, kind="ExternalInput").ap()
    w1 = nc.dram_tensor("w1", [NEXP, D, DFF], F32, kind="ExternalInput").ap()
    b1 = nc.dram_tensor("b1", [NEXP, DFF], F32, kind="ExternalInput").ap()
    w2 = nc.dram_tensor("w2", [NEXP, DFF, D], F32, kind="ExternalInput").ap()
    b2 = nc.dram_tensor("b2", [NEXP, D], F32, kind="ExternalInput").ap()
    out = nc.dram_tensor("out", [NTOK, D], F32, kind="ExternalOutput").ap()
    if DEBUG:
        dbg_x1 = nc.dram_tensor("dbg_x1", [128, NT, D], F32, kind="ExternalOutput").ap()
        dbg_c = nc.dram_tensor("dbg_c", [128, NT, NEXP], F32, kind="ExternalOutput").ap()
        dbg_q = nc.dram_tensor("dbg_q", [128, 8, NTOK], F32, kind="ExternalOutput").ap()
        dbg_k = nc.dram_tensor("dbg_k", [128, NGRP, 128 * NKT], F32, kind="ExternalOutput").ap()
        dbg_v = nc.dram_tensor("dbg_v", [128, NGRP, NKT * 65], F32, kind="ExternalOutput").ap()
        dbg_bias = nc.dram_tensor("dbg_bias", [128, NHEAD], F32, kind="ExternalOutput").ap()

    with contextlib.ExitStack() as ctx:
        E = ctx.enter_context
        x_sb = E(nc.sbuf_tensor("x_sb", [128, NT, D], F32))
        bo_bc = E(nc.sbuf_tensor("bo_bc", [128, D], BF16))
        cT = E(nc.sbuf_tensor("cT", [8, NTOK], BF16))
        c_sb = E(nc.sbuf_tensor("c_sb", [128, NT, NEXP], F32))
        sums = E(nc.sbuf_tensor("sums", [128, 2 * NT], F32))
        sumsq = E(nc.sbuf_tensor("sumsq", [128, 2 * NT], F32))
        mean = E(nc.sbuf_tensor("mean", [128, 2 * NT], F32))
        rstd = E(nc.sbuf_tensor("rstd", [128, 2 * NT], F32))
        nmr = E(nc.sbuf_tensor("nmr", [128, 2 * NT], F32))
        var_ = E(nc.sbuf_tensor("var_", [128, 2 * NT], F32))
        bias_h = E(nc.sbuf_tensor("bias_h", [128, NHEAD], F32))
        bias_p = E(nc.sbuf_tensor("bias_p", [128, 8], F32))
        bq_sb = E(nc.sbuf_tensor("bq_sb", [128, 8], F32))
        bk_sb = E(nc.sbuf_tensor("bk_sb", [128, 2], F32))
        idn = E(nc.sbuf_tensor("idn", [128, 128], BF16))
        idn32 = E(nc.sbuf_tensor("idn32", [128, 128], F32))
        iot = E(nc.sbuf_tensor("iot", [128, 128], F32))
        ones = E(nc.sbuf_tensor("ones", [1, 128], BF16))
        gw_sb = E(nc.sbuf_tensor("gw_sb", [128, 8, NEXP], BF16))
        gb_sb = E(nc.sbuf_tensor("gb_sb", [1, NEXP], BF16))
        bv_sb = E(nc.sbuf_tensor("bv_sb", [1, 256], BF16))
        bo_sb = E(nc.sbuf_tensor("bo_sb", [1, D], BF16))
        b2_sb = E(nc.sbuf_tensor("b2_sb", [NEXP, D], BF16))
        qmaxsq = E(nc.sbuf_tensor("qmaxsq", [2, 4], F32))
        qmx = E(nc.sbuf_tensor("qmx", [2, 1], F32))
        kmaxsq = E(nc.sbuf_tensor("kmaxsq", [1, 8], F32))
        kg2 = E(nc.sbuf_tensor("kg2", [2, NGRP], F32))
        mneg = E(nc.sbuf_tensor("mneg", [2, 1], BF16))
        halfone = E(nc.sbuf_tensor("halfone", [128, 2], BF16))
        eps_sb = E(nc.sbuf_tensor("eps_sb", [128, 1], F32))
        sel2 = E(nc.sbuf_tensor("sel2", [2, 2, 128], BF16))
        iotp2 = E(nc.sbuf_tensor("iotp2", [2, 128], F32))
        kg16 = E(nc.sbuf_tensor("kg16", [1, NGRP], BF16))
        gsc = E(nc.sbuf_tensor("gsc", [128, 4, 6], F32))
        gmsk = E(nc.sbuf_tensor("gmsk", [128, 4, 3, NEXP], F32))

        prog_sem = {e: E(nc.semaphore(f"pg_{e}")) for e in ENGS}
        dma_sem = {e: [E(nc.semaphore(f"dm_{e}{i}")) for i in range(Planner.NLANE)]
                   for e in ENGS}

        t_iota = add("gpsimd", lambda g: g.iota(
            iot[:, :], [[1, 128]], base=0, channel_multiplier=-1,
            allow_small_or_imprecise_dtypes=True))
        t_idn = add("vector", lambda v: v.tensor_single_scalar(
            idn[:, :], iot[:, :], 0.0, ALU.is_equal), deps=[t_iota])
        t_idn32 = add("vector", lambda v: v.tensor_single_scalar(
            idn32[:, :], iot[:, :], 0.0, ALU.is_equal), deps=[t_iota])
        t_ones = add("vector", lambda v: v.memset(ones[:, :], 1.0))
        t_eps = add("vector", lambda v: v.memset(eps_sb[:, :], 1e-5))
        t_h0 = add("vector", lambda v: v.memset(halfone[:, :], 0.0))
        t_h1 = add("vector", lambda v: v.memset(halfone[0:64, 0:1], 1.0), deps=[t_h0])
        t_h2 = add("vector", lambda v: v.memset(halfone[64:128, 1:2], 1.0), deps=[t_h0])
        t_iotp = add("gpsimd", lambda g: g.iota(
            iotp2[:, :], [[0, 128]], base=0, channel_multiplier=1,
            allow_small_or_imprecise_dtypes=True))
        t_s1 = add("vector", lambda v: v.tensor_single_scalar(
            sel2[:, 0, :], iotp2[:, :], 0.0, ALU.is_equal), deps=[t_iotp])
        t_s2 = add("vector", lambda v: v.tensor_single_scalar(
            sel2[:, 1, :], iotp2[:, :], 1.0, ALU.is_equal), deps=[t_iotp])

        t_xown = add("sync", lambda s: s.dma_start(
            x_sb[:, :, :], xo.rearrange("(t p) j -> p t j", p=128)), dma=True)


        with contextlib.ExitStack() as actx:
            E = actx.enter_context
            kT = E(nc.sbuf_tensor("kT", [128, NGRP, 128 * NKT], BF16))
            v_sb = E(nc.sbuf_tensor("v_sb", [128, NGRP, NKT * 65], BF16))
            qT = E(nc.sbuf_tensor("qT", [128, 8, NTOK], BF16))
            t_vones = add("vector", lambda v: v.memset(v_sb[:, :, :], 1.0))

            bias_tok = {}

            with contextlib.ExitStack() as bctx:
                Eb = bctx.enter_context
                wq_sb = Eb(nc.sbuf_tensor("wq_sb", [128, 8, D], BF16))
                wk_sb = Eb(nc.sbuf_tensor("wk_sb", [128, 8, 256], BF16))
                wv_sb = Eb(nc.sbuf_tensor("wv_sb", [128, 8, 256], BF16))
                x_oth = Eb(nc.sbuf_tensor("xoth_sb", [128, 2, D], F32))
                ln_scr = Eb(nc.sbuf_tensor("ln_scr", [128, 6, D], BF16))
                lnT_scr = Eb(nc.sbuf_tensor("lnT_scr", [128, 2, 8, 128], BF16))
                normsq = nc.alloc_sbuf_tensor_at(
                    "normsq", [128, NTOK], BF16, offset=nc.lookup_mloc(ln_scr).addr)
                bqt = nc.alloc_sbuf_tensor_at(
                    "bqt", [8, 128], F32, offset=nc.lookup_mloc(ln_scr).addr + 2048)
                bkt = nc.alloc_sbuf_tensor_at(
                    "bkt", [2, 128], F32, offset=nc.lookup_mloc(ln_scr).addr + 2560)
                ptr = [Eb(nc.psum_tensor(f"ptr{i}", [128, 128], BF16)) for i in range(2)]
                pq = [Eb(nc.psum_tensor(f"pq{i}", [128, 128], F32)) for i in range(2)]
                pk = Eb(nc.psum_tensor("pk", [128, 256], F32))
                pvp = Eb(nc.psum_tensor("pvp", [128, 256], F32))
                pnb = Eb(nc.psum_tensor("pnb", [128, 512], F32))

                t_bqt = add("sync", lambda s: s.dma_start(
                    bqt[:, :], bq.rearrange("(c p) -> c p", p=128)), dma=True)
                t_bkt = add("sync", lambda s: s.dma_start(
                    bkt[:, :], bk.rearrange("(c p) -> c p", p=128)), dma=True)
                t_trbq = add("tensor", lambda tn: tn.transpose(
                    pq[0][:, 0:8], bqt[:, :], idn32[0:8, 0:8]),
                    deps=[t_bqt, t_idn32])
                t_bq = add("vector", lambda v: v.tensor_copy(
                    bq_sb[:, :], pq[0][:, 0:8]), deps=[t_trbq])
                t_trbk = add("tensor", lambda tn: tn.transpose(
                    pq[1][:, 0:2], bkt[:, :], idn32[0:2, 0:2]),
                    deps=[t_bkt, t_idn32])
                t_bk = add("vector", lambda v: v.tensor_copy(
                    bk_sb[:, :], pq[1][:, 0:2]), deps=[t_trbk])
                t_wq = add("gpsimd", lambda g: g.dma_start(
                    wq_sb[:, :, :], wq.rearrange("(c p) j -> p c j", p=128)), dma=True)
                t_wk = add("gpsimd", lambda g: g.dma_start(
                    wk_sb[:, :, :], wk.rearrange("(c p) j -> p c j", p=128)), dma=True)
                t_wv = add("gpsimd", lambda g: g.dma_start(
                    wv_sb[:, :, :], wv.rearrange("(c p) j -> p c j", p=128)), dma=True)
                t_bv = add("gpsimd", lambda g: g.dma_start(
                    bv_sb[0:1, :], bv[None, :]), dma=True)
                t_gw = add("gpsimd", lambda g: g.dma_start(
                    gw_sb[:, :, :], gw.rearrange("(c p) j -> p c j", p=128)), dma=True)
                t_gb = add("gpsimd", lambda g: g.dma_start(
                    gb_sb[0:1, :], gb[None, :]), dma=True)
                t_bo = add("gpsimd", lambda g: g.dma_start(
                    bo_sb[0:1, :], bo[None, :]), dma=True)
                t_b2l = add("gpsimd", lambda g: g.dma_start(
                    b2_sb[:, :], b2[:, :]), dma=True)

                xoth_free = [None, None]
                lnT_free = [None, None]
                lnscr_free = [None] * 6
                ptr_free = [None, None]
                pq_free = [t_bq, t_bk]
                pk_free = [None]
                pv_free = [None]

                ORDER = []
                for k2 in range(NT):
                    ORDER.append((True, k2))
                    ORDER.append((False, k2))

                def srcs_of(gi):
                    own, t = ORDER[gi]
                    if own:
                        return x_sb[:, t, :]
                    return x_oth[:, t % 2, :]

                ctoks = {}

                def plan_chain(g0, ph):
                    batch = list(range(g0, g0 + 4))
                    if ph == 0:
                        for gi in batch:
                            own, t = ORDER[gi]
                            if not own:
                                ctoks[("src", gi)] = add(
                                    "sync", lambda sy, t=t: sy.dma_start(
                                        x_oth[:, t % 2, :],
                                        xr[128 * t:128 * t + 128, :]),
                                    deps=[xoth_free[t % 2]], dma=True)
                            else:
                                ctoks[("src", gi)] = t_xown
                        for gi in batch:
                            sc = (slice(None), slice(gi, gi + 1))
                            ctoks[("rs", gi)] = add("vector", lambda v, gi=gi, sc=sc: v.reduce_sum(
                                sums[sc], srcs_of(gi), axis=AX.X), deps=[ctoks[("src", gi)]])
                        for gi in batch:
                            sc = (slice(None), slice(gi, gi + 1))
                            slot = gi % 6
                            ctoks[("sq", gi)] = add("scalar", lambda ss, gi=gi, sc=sc, slot=slot: ss.activation(
                                ln_scr[:, slot, :], srcs_of(gi), AF.Square,
                                accum_out=sumsq[sc]),
                                deps=[ctoks[("src", gi)], lnscr_free[slot],
                                      t_bq, t_bk])
                    elif ph == 1:
                        for gi in batch:
                            sc = (slice(None), slice(gi, gi + 1))
                            ctoks[("m", gi)] = add("vector", lambda v, sc=sc: v.tensor_single_scalar(
                                mean[sc], sums[sc], 1.0 / D, ALU.mult), deps=[ctoks[("rs", gi)]])
                        for gi in batch:
                            sc = (slice(None), slice(gi, gi + 1))
                            ctoks[("s2", gi)] = add("vector", lambda v, sc=sc: v.tensor_single_scalar(
                                var_[sc], sumsq[sc], 1.0 / D, ALU.mult), deps=[ctoks[("sq", gi)]])
                        for gi in batch:
                            sc = (slice(None), slice(gi, gi + 1))
                            ctoks[("mm", gi)] = add("vector", lambda v, sc=sc: v.tensor_mul(
                                nmr[sc], mean[sc], mean[sc]), deps=[ctoks[("m", gi)]])
                        for gi in batch:
                            sc = (slice(None), slice(gi, gi + 1))
                            ctoks[("var", gi)] = add("vector", lambda v, sc=sc: v.tensor_sub(
                                var_[sc], var_[sc], nmr[sc]),
                                deps=[ctoks[("s2", gi)], ctoks[("mm", gi)]])
                    elif ph == 2:
                        for gi in batch:
                            sc = (slice(None), slice(gi, gi + 1))
                            ctoks[("std", gi)] = add("scalar", lambda ss, sc=sc: ss.activation(
                                rstd[sc], var_[sc], AF.Sqrt, bias=eps_sb[:, 0:1]),
                                deps=[ctoks[("var", gi)], t_eps])
                        for gi in batch:
                            sc = (slice(None), slice(gi, gi + 1))
                            ctoks[("rst", gi)] = add("vector", lambda v, sc=sc: v.reciprocal(
                                rstd[sc], rstd[sc]), deps=[ctoks[("std", gi)]])
                        for gi in batch:
                            sc = (slice(None), slice(gi, gi + 1))
                            ctoks[("nmr", gi)] = add("vector", lambda v, sc=sc: v.scalar_tensor_tensor(
                                nmr[sc], mean[sc], -1.0, rstd[sc], ALU.mult, ALU.mult),
                                deps=[ctoks[("rst", gi)]])
                    else:
                        for gi in batch:
                            sc = (slice(None), slice(gi, gi + 1))
                            slot = gi % 6
                            ctoks[("ln", gi)] = add("scalar", lambda ss, gi=gi, sc=sc, slot=slot: ss.activation(
                                ln_scr[:, slot, :], srcs_of(gi), AF.Identity,
                                bias=nmr[sc], scale=rstd[sc]),
                                deps=[ctoks[("nmr", gi)], lnscr_free[slot]])
                            own, t = ORDER[gi]
                            if not own:
                                xoth_free[t % 2] = [ctoks[("ln", gi)], ctoks[("rs", gi)],
                                                    ctoks[("sq", gi)]]

                for ph in range(4):
                    plan_chain(0, ph)

                for g0 in range(0, 2 * NT, 4):
                    batch = list(range(g0, g0 + 4))
                    nxt = g0 + 4
                    for ti, gi in enumerate(batch):
                        own, t = ORDER[gi]
                        gt = t if own else NT + t
                        par = gi % 6
                        t_ln = ctoks[("ln", gi)]
                        ev_toks = []
                        t_tr = None
                        for c in range(8):
                            pp = c % 2
                            t_tr = add("tensor", lambda tn, c=c, par=par, pp=pp: tn.transpose(
                                ptr[pp][:, :], ln_scr[:, par, 128 * c:128 * c + 128],
                                idn[:, :]),
                                deps=[t_ln, t_idn, ptr_free[pp], lnT_free[gi % 2]])
                            t_ev = add("vector", lambda v, c=c, gi=gi, pp=pp: v.tensor_copy(
                                lnT_scr[:, gi % 2, c, :], ptr[pp][:, :]), deps=[t_tr])
                            ptr_free[pp] = t_ev
                            ev_toks.append(t_ev)
                        lnscr_free[par] = t_tr
                        lnT = lnT_scr[:, gi % 2]

                        kmm = None
                        for kc in range(2):
                            for ci in range(8):
                                dd = [t_wk, pk_free[0]] + ev_toks if (kc == 0 and ci == 0) else [ev_toks[ci]]
                                kmm = add("tensor", lambda tn, kc=kc, ci=ci, lnT=lnT: tn.matmul(
                                    pk[:, 128 * kc:128 * kc + 128],
                                    wk_sb[:, ci, 128 * kc:128 * kc + 128],
                                    lnT[:, ci, :],
                                    start=(kc == 0 and ci == 0),
                                    stop=(kc == 1 and ci == 7)), deps=dd)
                        k_ev = None
                        for g in range(NGRP):
                            kc, hh = g // 2, g % 2
                            for dup in range(2):
                                k_ev = add("scalar", lambda ss, g=g, kc=kc, hh=hh, dup=dup, gt=gt: ss.activation(
                                    kT[64 * dup:64 * dup + 64, g, 128 * gt:128 * gt + 128],
                                    pk[64 * hh:64 * hh + 64, 128 * kc:128 * kc + 128],
                                    AF.Identity, bias=bk_sb[64 * hh:64 * hh + 64, kc:kc + 1]),
                                    deps=[kmm, t_bk])
                        pk_free[0] = k_ev

                        vmm = None
                        for ci in range(8):
                            dd = [t_wv, pv_free[0], ev_toks[0]] if ci == 0 else [ev_toks[ci]]
                            vmm = add("tensor", lambda tn, ci=ci, lnT=lnT: tn.matmul(
                                pvp[:, :], lnT[:, ci, :], wv_sb[:, ci, :],
                                start=(ci == 0), stop=False), deps=dd)
                        vmm = add("tensor", lambda tn: tn.matmul(
                            pvp[:, :], ones[0:1, :], bv_sb[0:1, :],
                            start=False, stop=True), deps=[t_bv, t_ones])
                        v_ev = None
                        for g in range(NGRP):
                            v_ev = add("vector", lambda v, g=g, gt=gt: v.tensor_copy(
                                v_sb[:, g, 65 * gt:65 * gt + 64],
                                pvp[:, 64 * g:64 * g + 64]), deps=[vmm, t_vones])
                        pv_free[0] = v_ev
                        last_cons = v_ev

                        if own:
                            for j in range(8):
                                pp = j % 2
                                qmm = None
                                for ci in range(8):
                                    dd = [t_wq, pq_free[pp], ev_toks[0]] if ci == 0 else [ev_toks[ci]]
                                    qmm = add("tensor", lambda tn, j=j, ci=ci, pp=pp, lnT=lnT: tn.matmul(
                                        pq[pp][:, :], wq_sb[:, ci, 128 * j:128 * j + 128],
                                        lnT[:, ci, :], start=(ci == 0), stop=(ci == 7)),
                                        deps=dd)
                                q_ev = None
                                for hh in range(2):
                                    h = 2 * j + hh
                                    pr, half = PAIR_OF[h]
                                    q_ev = add("scalar", lambda ss, hh=hh, pr=pr, half=half, t=t, pp=pp, j=j: ss.activation(
                                        qT[64 * half:64 * half + 64, pr, 128 * t:128 * t + 128],
                                        pq[pp][64 * hh:64 * hh + 64, :], AF.Identity,
                                        bias=bq_sb[64 * hh:64 * hh + 64, j:j + 1]),
                                        deps=[qmm, t_bq])
                                pq_free[pp] = q_ev
                                last_cons = q_ev
                        lnT_free[gi % 2] = last_cons
                        if nxt < 2 * NT:
                            plan_chain(nxt, ti)

                b_exit_b1 = P.barrier_tokens()

                pnb_free = [None]
                normsq_free = [None]
                for g in range(NGRP):
                    mx_toks = []
                    for hr in range(2):
                        t_ksq = add("scalar", lambda s, g=g, hr=hr: s.activation(
                            normsq[0:64, :], kT[0:64, g, 2048 * hr:2048 * hr + 2048],
                            AF.Square), deps=b_exit_b1 + [normsq_free[0]])
                        for i in range(4):
                            t_nm = add("tensor", lambda tn, i=i: tn.matmul(
                                pnb[0:1, :], halfone[0:64, 0:1],
                                normsq[0:64, 512 * i:512 * i + 512],
                                start=True, stop=True),
                                deps=[t_ksq, t_h1, pnb_free[0]] + (b_exit_b1 if (g == 0 and hr == 0 and i == 0) else []))
                            normsq_free[0] = t_nm
                            t_mx = add("vector", lambda v, i=i, hr=hr: v.reduce_max(
                                kmaxsq[0:1, 4 * hr + i:4 * hr + i + 1], pnb[0:1, :],
                                axis=AX.X), deps=[t_nm])
                            pnb_free[0] = t_mx
                            mx_toks.append(t_mx)
                    t_km = add("vector", lambda v, g=g: v.reduce_max(
                        kg2[0:1, g:g + 1], kmaxsq[0:1, :], axis=AX.X), deps=mx_toks)
                    t_ks = add("scalar", lambda s, g=g: s.activation(
                        kg2[0:1, g:g + 1], kg2[0:1, g:g + 1], AF.Sqrt), deps=[t_km])
                    t_k16 = add("vector", lambda v, g=g: v.tensor_copy(
                        kg16[0:1, g:g + 1], kg2[0:1, g:g + 1]), deps=[t_ks])
                    t_kbc = add("tensor", lambda tn, g=g: tn.matmul(
                        pnb[0:2, 0:1], ones[0:1, 0:2], kg16[0:1, g:g + 1],
                        start=True, stop=True),
                        deps=[t_k16, pnb_free[0], t_ones])
                    t_kcp = add("vector", lambda v, g=g: v.tensor_copy(
                        kg2[0:2, g:g + 1], pnb[0:2, 0:1]), deps=[t_kbc])
                    pnb_free[0] = t_kcp
                kg_done = P.last["vector"]

                mneg_free = [None]
                for pr in range(8):
                    g = PAIRS[pr][0] % NGRP
                    t_qsq = add("scalar", lambda s, pr=pr: s.activation(
                        normsq[:, 0:NTOK], qT[:, pr, :], AF.Square),
                        deps=[normsq_free[0]])
                    qm_toks = []
                    for c in range(4):
                        t_nm = add("tensor", lambda tn, c=c: tn.matmul(
                            pnb[0:2, :], halfone[:, :],
                            normsq[:, 512 * c:512 * c + 512],
                            start=True, stop=True),
                            deps=[t_qsq, t_h1, t_h2, pnb_free[0]])
                        normsq_free[0] = t_nm
                        t_mx = add("vector", lambda v, c=c: v.reduce_max(
                            qmaxsq[0:2, c:c + 1], pnb[0:2, :], axis=AX.X),
                            deps=[t_nm])
                        pnb_free[0] = t_mx
                        qm_toks.append(t_mx)
                    t_qm = add("vector", lambda v: v.reduce_max(
                        qmx[0:2, 0:1], qmaxsq[0:2, :], axis=AX.X), deps=qm_toks)
                    t_qs = add("scalar", lambda s: s.activation(
                        qmx[0:2, 0:1], qmx[0:2, 0:1], AF.Sqrt), deps=[t_qm])
                    t_mn = add("vector", lambda v, g=g: v.scalar_tensor_tensor(
                        mneg[0:2, 0:1], qmx[0:2, 0:1], -0.125, kg2[0:2, g:g + 1],
                        ALU.mult, ALU.mult), deps=[t_qs, kg_done, mneg_free[0]])
                    for half in range(2):
                        h = PAIRS[pr][half]
                        t_bc = add("tensor", lambda tn, half=half: tn.matmul(
                            pnb[:, 0:1], sel2[:, half, :], mneg[0:2, 0:1],
                            start=True, stop=True),
                            deps=[t_mn, pnb_free[0], t_s1, t_s2])
                        mneg_free[0] = t_bc
                        t_cp = add("vector", lambda v, h=h: v.tensor_copy(
                            bias_h[:, h:h + 1], pnb[:, 0:1]), deps=[t_bc])
                        pnb_free[0] = t_cp
                        bias_tok[h] = t_cp
                bias_ptok = {}
                for pr2 in range(8):
                    hA2, hB2 = PAIRS[pr2]
                    bias_ptok[pr2] = add("vector", lambda v, pr2=pr2, hA2=hA2, hB2=hB2: v.tensor_tensor(
                        bias_p[:, pr2:pr2 + 1], bias_h[:, hA2:hA2 + 1],
                        bias_h[:, hB2:hB2 + 1], ALU.min),
                        deps=[bias_tok[hA2], bias_tok[hB2]])

                bo_ready = []
                for dh in range(2):
                    t_bm0 = add("tensor", lambda tn, dh=dh: tn.matmul(
                        pnb[:, :], ones[0:1, :],
                        bo_sb[0:1, 512 * dh:512 * dh + 512],
                        start=True, stop=True),
                        deps=[t_bo, t_ones, pnb_free[0]])
                    t_cp0 = add("vector", lambda v, dh=dh: v.tensor_copy(
                        bo_bc[:, 512 * dh:512 * dh + 512], pnb[:, :]),
                        deps=[t_bm0])
                    pnb_free[0] = t_cp0
                    bo_ready.append(t_cp0)

                b_exit = P.barrier_tokens()

            with contextlib.ExitStack() as cctx:
                Ec = cctx.enter_context
                wo_sb = Ec(nc.sbuf_tensor("wo_sb", [128, 2, D], BF16))
                rb_sb = Ec(nc.sbuf_tensor("rb_sb", [64, 512], F32))
                recip16 = Ec(nc.sbuf_tensor("recip16", [1, 512], BF16))
                attn_pr = Ec(nc.sbuf_tensor("attn_pr", [128, QB], BF16))
                p_sb = Ec(nc.sbuf_tensor("p_sb", [128, 16 * 1024], BF16))
                sq2a = Ec(nc.sbuf_tensor("sq2a", [128, D], BF16))
                sAB = [Ec(nc.psum_tensor(f"sAB{i}", [128, 1024], F32)) for i in range(2)]
                pvb = [Ec(nc.psum_tensor(f"pvb{i}", [128, 512], F32)) for i in range(2)]
                ob = [Ec(nc.psum_tensor(f"ob{i}", [128, 512], F32)) for i in range(2)]

                wo_tok = [None, None]
                wo_free = [None, None]
                sc_free = {}
                pv_slot_prev = {}
                norm_done = [None, None]
                ob_free = [None, None]
                oadd_tok = {}
                it = 0
                last_oadd = None

                def plan_norm(cx):
                    par, wpar, qb = cx["par"], cx["wpar"], cx["qb"]
                    pv_mm = cx["pv_mm"]
                    t_rc = add("vector", lambda v, par=par: v.reciprocal(
                        rb_sb[0:1, 0:512], pvb[par][64:65, :]), deps=[pv_mm])
                    t_r16 = add("vector", lambda v: v.tensor_copy(
                        recip16[0:1, :], rb_sb[0:1, 0:512]), deps=[t_rc])
                    at_toks = []
                    for half in range(2):
                        t_bc = add("tensor", lambda tn, half=half: tn.matmul(
                            ob[half][0:64, 0:256], ones[0:1, 0:64],
                            recip16[0:1, 256 * half:256 * half + 256],
                            start=True, stop=True),
                            deps=[t_r16, ob_free[half]])
                        t_rb = add("vector", lambda v, half=half: v.tensor_copy(
                            rb_sb[0:64, 256 * half:256 * half + 256],
                            ob[half][0:64, 0:256]), deps=[t_bc])
                        ob_free[half] = t_rb
                        t_at = add("vector", lambda v, half=half, par=par: v.tensor_mul(
                            attn_pr[64 * half:64 * half + 64, :],
                            pvb[par][0:64, 256 * half:256 * half + 256],
                            rb_sb[0:64, 256 * half:256 * half + 256]),
                            deps=[t_rb])
                        at_toks.append(t_at)
                    norm_done[par] = at_toks[-1]
                    for tt in range(2):
                        xt = 2 * qb + tt
                        for dh in range(2):
                            op = (2 * tt + dh) % 2
                            t_om = add("tensor", lambda tn, tt=tt, dh=dh, op=op, wpar=wpar: tn.matmul(
                                ob[op][:, :], attn_pr[:, 128 * tt:128 * tt + 128],
                                wo_sb[:, wpar, 512 * dh:512 * dh + 512],
                                start=True, stop=True),
                                deps=at_toks + [wo_tok[wpar], ob_free[op]])
                            t_oa = add("vector", lambda v, xt=xt, dh=dh, op=op: v.tensor_add(
                                x_sb[:, xt, 512 * dh:512 * dh + 512],
                                x_sb[:, xt, 512 * dh:512 * dh + 512],
                                ob[op][:, :]), deps=[t_om, oadd_tok.get((xt, dh))])
                            ob_free[op] = t_oa
                            oadd_tok[(xt, dh)] = t_oa
                    wo_free[wpar] = P.last["tensor"]

                ln2tk = {}
                bo_add_tok = {}

                def plan_x1_final(qb):
                    for tt2 in range(2):
                        t = 2 * qb + tt2
                        t_ba = add("vector", lambda v, t=t: v.tensor_add(
                            x_sb[:, t, :], x_sb[:, t, :], bo_bc[:, :]),
                            deps=bo_ready + [oadd_tok[(t, 0)], oadd_tok[(t, 1)]])
                        bo_add_tok[t] = t_ba
                        ln2tk[("rs", t)] = add("vector", lambda v, t=t: v.reduce_sum(
                            sums[:, t:t + 1], x_sb[:, t, :], axis=AX.X),
                            deps=[t_ba])
                        ln2tk[("sq", t)] = add("scalar", lambda ss, t=t: ss.activation(
                            sq2a[:, :], x_sb[:, t, :], AF.Square,
                            accum_out=sumsq[:, t:t + 1]), deps=[t_ba])
                        ln2tk[("m", t)] = add("vector", lambda v, t=t: v.tensor_single_scalar(
                            mean[:, t:t + 1], sums[:, t:t + 1], 1.0 / D, ALU.mult),
                            deps=[ln2tk[("rs", t)]])
                        ln2tk[("s2", t)] = add("vector", lambda v, t=t: v.tensor_single_scalar(
                            var_[:, t:t + 1], sumsq[:, t:t + 1], 1.0 / D, ALU.mult),
                            deps=[ln2tk[("sq", t)]])
                        ln2tk[("mm", t)] = add("vector", lambda v, t=t: v.tensor_mul(
                            nmr[:, t:t + 1], mean[:, t:t + 1], mean[:, t:t + 1]),
                            deps=[ln2tk[("m", t)]])
                        ln2tk[("var", t)] = add("vector", lambda v, t=t: v.tensor_sub(
                            var_[:, t:t + 1], var_[:, t:t + 1], nmr[:, t:t + 1]),
                            deps=[ln2tk[("s2", t)], ln2tk[("mm", t)]])
                        ln2tk[("std", t)] = add("scalar", lambda ss, t=t: ss.activation(
                            rstd[:, t:t + 1], var_[:, t:t + 1], AF.Sqrt,
                            bias=eps_sb[:, 0:1]), deps=[ln2tk[("var", t)], t_eps])
                        ln2tk[("rst", t)] = add("vector", lambda v, t=t: v.reciprocal(
                            rstd[:, t:t + 1], rstd[:, t:t + 1]),
                            deps=[ln2tk[("std", t)]])
                        ln2tk[("nmr", t)] = add("vector", lambda v, t=t: v.scalar_tensor_tensor(
                            nmr[:, t:t + 1], mean[:, t:t + 1], -1.0, rstd[:, t:t + 1],
                            ALU.mult, ALU.mult), deps=[ln2tk[("rst", t)]])

                pend = [None]
                for pr in range(8):
                    hA, hB = PAIRS[pr]
                    g = hA % NGRP
                    wpar = pr % 2
                    wo_both = []
                    for half, h in ((0, hA), (1, hB)):
                        wo_both.append(add("gpsimd", lambda gg, h=h, half=half, wpar=wpar: gg.dma_start(
                            wo_sb[64 * half:64 * half + 64, wpar, :],
                            wo[64 * h:64 * h + 64, :]),
                            deps=[wo_free[wpar]] + (b_exit if pr < 2 else []),
                            dma=True))
                    wo_tok[wpar] = wo_both
                    for qb in range(NQB):
                        par = it % 2
                        exp_toks = {}
                        pv_mm = None
                        pv_this = {}

                        def plan_scores(s):
                            sp = s % 2
                            mm = {}
                            for u in range(2):
                                kt = 2 * s + u
                                dd0 = [sc_free.get(sp)] if u == 0 else []
                                if it == 0 and s < 2:
                                    dd0 = dd0 + b_exit
                                mm[0] = add("tensor", lambda tn, kt=kt, u=u, sp=sp, g=g, pr=pr, qb=qb: tn.matmul(
                                    sAB[sp][:, 256 * u:256 * u + 256],
                                    kT[0:64, g, 128 * kt:128 * kt + 128],
                                    qT[0:64, pr, QB * qb:QB * qb + QB],
                                    start=(u == 0), stop=(u == 1),
                                    tile_position=(0, 0)), deps=dd0)
                                mm[1] = add("tensor", lambda tn, kt=kt, u=u, sp=sp, g=g, pr=pr, qb=qb: tn.matmul(
                                    sAB[sp][:, 512 + 256 * u:512 + 256 * u + 256],
                                    kT[64:128, g, 128 * kt:128 * kt + 128],
                                    qT[64:128, pr, QB * qb:QB * qb + QB],
                                    start=(u == 0), stop=(u == 1),
                                    tile_position=(64, 0)),
                                    deps=([sc_free.get(sp)] if u == 0 else []))
                            dd = [mm[0], mm[1], bias_ptok[pr],
                                  pv_slot_prev.get((s, 0, 0)),
                                  pv_slot_prev.get((s, 0, 1)),
                                  pv_slot_prev.get((s, 1, 0)),
                                  pv_slot_prev.get((s, 1, 1))]
                            t_e = add("scalar", lambda sc_, sp=sp, pr=pr, s=s: sc_.activation(
                                p_sb[:, 1024 * s:1024 * s + 1024], sAB[sp][:, :],
                                AF.Exp, bias=bias_p[:, pr:pr + 1], scale=0.125),
                                deps=dd)
                            exp_toks[(s, 0)] = t_e
                            exp_toks[(s, 1)] = t_e
                            sc_free[sp] = t_e

                        def plan_pv(s):
                            nonlocal pv_mm
                            for half in range(2):
                                for u in range(2):
                                    kt = 2 * s + u
                                    st = (s == 0 and half == 0 and u == 0)
                                    sp_ = (s == 15 and half == 1 and u == 1)
                                    dd = [exp_toks[(s, half)]]
                                    if st:
                                        dd += [norm_done[par], t_vones]
                                    pv_mm = add("tensor", lambda tn, kt=kt, half=half, st=st, sp_=sp_, par=par, g=g, s=s: tn.matmul(
                                        pvb[par][0:65, 256 * half:256 * half + 256],
                                        v_sb[:, g, 65 * kt:65 * kt + 65],
                                        p_sb[:, 1024 * s + 512 * half + 256 * (kt % 2):1024 * s + 512 * half + 256 * (kt % 2) + 256],
                                        start=st, stop=sp_), deps=dd)
                                    pv_this[(s, half, u)] = pv_mm

                        for s in range(16):
                            plan_scores(s)
                            if s == 5 and pend[0] is not None:
                                cx0 = pend[0]
                                plan_norm(cx0)
                                pend[0] = None
                                if cx0["pr"] == 7:
                                    plan_x1_final(cx0["qb"])
                            if s >= 2:
                                plan_pv(s - 2)
                        plan_pv(14)
                        plan_pv(15)
                        pv_slot_prev = pv_this
                        pend[0] = {"par": par, "wpar": wpar, "qb": qb,
                                   "pv_mm": pv_mm, "pr": pr}
                        it += 1
                plan_norm(pend[0])
                plan_x1_final(pend[0]["qb"])
                pend[0] = None

                if DEBUG:
                    add("gpsimd", lambda g: g.dma_start(dbg_q[:, :, :], qT[:, :, :]),
                        deps=[bo_add_tok[NT - 1]], dma=True)
                    add("gpsimd", lambda g: g.dma_start(dbg_k[:, :, :], kT[:, :, :]),
                        dma=True)
                    add("gpsimd", lambda g: g.dma_start(dbg_v[:, :, :], v_sb[:, :, :]),
                        dma=True)
                    add("sync", lambda s: s.dma_start(dbg_bias[:, :], bias_h[:, :]),
                        deps=[bo_add_tok[NT - 1]], dma=True)
                    add("sync", lambda s: s.dma_start(dbg_x1[:, :, :], x_sb[:, :, :]),
                        deps=[bo_add_tok[t2] for t2 in range(NT)], dma=True)

            c_exit = P.barrier_tokens()

        CAP = 384
        NTT = 8
        NSC = CAP // 128
        BIG = 1.0e6
        with contextlib.ExitStack() as dctx:
            Ed = dctx.enter_context
            ln2N = Ed(nc.sbuf_tensor("ln2N", [128, NT, D], BF16))
            h_sb = Ed(nc.sbuf_tensor("h_sb", [128, 2, 4, CAP], BF16))
            w1_sb = Ed(nc.sbuf_tensor("w1_sb", [128, 2, 8, 256], BF16))
            w2_sb = Ed(nc.sbuf_tensor("w2_sb", [128, 2, 4, D], BF16))
            b1r = Ed(nc.sbuf_tensor("b1r", [1, 2, 256], BF16))
            sq2 = Ed(nc.sbuf_tensor("sq2", [128, D], BF16))
            lnTg = Ed(nc.sbuf_tensor("lnTg", [128, 2, 8, CAP], BF16))
            c_g = Ed(nc.sbuf_tensor("c_g", [1, 2, CAP], BF16))
            pmat = Ed(nc.sbuf_tensor("pmat", [128, 2, NTT, CAP], BF16))
            p0scr = Ed(nc.sbuf_tensor("p0scr", [128, 2, CAP], BF16))
            p0s = Ed(nc.sbuf_tensor("p0s", [128, 2, NSC, 1024], BF16))
            yacc = Ed(nc.sbuf_tensor("yacc", [128, 2, NSC, D], F32))
            rank_sb = Ed(nc.sbuf_tensor("rank_sb", [128, NT], F32))
            msk16 = Ed(nc.sbuf_tensor("msk16", [128, NT], BF16))
            umask = Ed(nc.sbuf_tensor("umask", [128, NT], F32))

            ones_c = Ed(nc.sbuf_tensor("ones_c", [128, 1], BF16))
            iotaS = nc.alloc_sbuf_tensor_at(
                "iotaS", [128, CAP], F32, offset=nc.lookup_mloc(sq2).addr)
            y16 = nc.alloc_sbuf_tensor_at(
                "y16", [128, 2, NSC, D], BF16, offset=nc.lookup_mloc(pmat).addr)
            lnT2 = nc.alloc_sbuf_tensor_at(
                "lnT2", [128, 2, 8, 128], BF16, offset=nc.lookup_mloc(h_sb).addr)
            ph = [Ed(nc.psum_tensor(f"ph{i}", [128, 512], F32)) for i in range(2)]
            py = [Ed(nc.psum_tensor(f"py{i}", [128, 512], F32)) for i in range(2)]
            ptr2 = [Ed(nc.psum_tensor(f"ptr2_{i}", [128, 128], BF16)) for i in range(2)]
            pg = Ed(nc.psum_tensor("pg", [128, 512], F32))
            pgt = Ed(nc.psum_tensor("pgt", [128, 512], F32))

            t_oc = add("vector", lambda v: v.memset(ones_c[:, :], 1.0))


            w1dma = {}
            w2dma = {}
            pair_done = {}

            def plan_w1_dma(m, half, dep):
                if m >= 8 * NEXP:
                    return
                e, mj = divmod(m, 8)
                j = 2 * mj + half
                t_a = add("gpsimd", lambda g, e=e, j=j, half=half: g.dma_start(
                    w1_sb[:, half, :, :],
                    w1[e, :, 256 * j:256 * j + 256].rearrange(
                        "(c p) f -> p c f", p=128)), deps=dep, dma=True)
                t_b = add(
                    "gpsimd", lambda g, e=e, j=j, half=half: g.dma_start(
                        b1r[0:1, half, :],
                        b1[e, 256 * j:256 * j + 256][None, :]), dma=True)
                w1dma[2 * m + half] = [t_a, t_b]

            def plan_w2_dma(m, dep):
                if m >= 8 * NEXP:
                    return
                e, mj = divmod(m, 8)
                w2dma[m] = add("gpsimd", lambda g, e=e, mj=mj, m=m: g.dma_start(
                    w2_sb[:, m % 2, :, :],
                    w2[e, 512 * mj:512 * mj + 512, :].rearrange(
                        "(c p) f -> p c f", p=128)), deps=dep, dma=True)

            plan_w1_dma(0, 0, c_exit)
            plan_w1_dma(0, 1, [])
            plan_w2_dma(0, [])
            plan_w2_dma(1, [])

            ptr2_free = [None, None]
            lnT2_free = [None, None]
            gbanks = None
            gfree4 = [[None], [None], [None], [None]]
            c_tok = {}
            cT_toks = []
            ln2N_toks = {}
            for g0 in range(0, NT, 4):
                batch = list(range(g0, g0 + 4))
                tk = {}
                for t in batch:
                    dd = [ln2tk[("nmr", t)]] + (c_exit if t == 0 else [])
                    tk[("ln", t)] = add("scalar", lambda ss, t=t: ss.activation(
                        ln2N[:, t, :], x_sb[:, t, :], AF.Identity,
                        bias=nmr[:, t:t + 1], scale=rstd[:, t:t + 1]),
                        deps=dd)
                    ln2N_toks[t] = tk[("ln", t)]
                for t in batch:
                    t_ln = tk[("ln", t)]
                    ev_toks = []
                    for c in range(8):
                        pp = c % 2
                        t_tr = add("tensor", lambda tn, c=c, pp=pp, t=t: tn.transpose(
                            ptr2[pp][:, :], ln2N[:, t, 128 * c:128 * c + 128],
                            idn[:, :]), deps=[t_ln, ptr2_free[pp], t_idn,
                                              lnT2_free[t % 2]])
                        t_ev = add("vector", lambda v, c=c, t=t, pp=pp: v.tensor_copy(
                            lnT2[:, t % 2, c, :], ptr2[pp][:, :]), deps=[t_tr])
                        ptr2_free[pp] = t_ev
                        ev_toks.append(t_ev)
                    gbk = [pg, pgt, py[0], py[1]][t % 4]
                    gfree = gfree4[t % 4]
                    gmm = None
                    for ci in range(8):
                        dd2 = [t_gw, gfree[0], ev_toks[ci]] if ci == 0 else [ev_toks[ci]]
                        gmm = add("tensor", lambda tn, ci=ci, t=t, gbk=gbk: tn.matmul(
                            gbk[:, 0:NEXP], lnT2[:, t % 2, ci, :],
                            gw_sb[:, ci, :],
                            start=(ci == 0), stop=False), deps=dd2)
                    gmm = add("tensor", lambda tn, gbk=gbk: tn.matmul(
                        gbk[:, 0:NEXP], ones[0:1, 0:128],
                        gb_sb[0:1, :], start=False, stop=True),
                        deps=[t_gb, t_ones])
                    lnT2_free[t % 2] = gmm
                    tk[("gmm", t)] = gmm
                q = {}
                for t in batch:
                    sl = t % 4
                    gbk = [pg, pgt, py[0], py[1]][t % 4]
                    q[("g1", t)] = add("vector", lambda v, sl=sl, gbk=gbk: v.reduce_max(
                        gsc[:, sl, 0:1], gbk[:, 0:NEXP], axis=AX.X),
                        deps=[tk[("gmm", t)]])
                for t in batch:
                    sl = t % 4
                    gbk = [pg, pgt, py[0], py[1]][t % 4]
                    q[("k1", t)] = add("vector", lambda v, sl=sl, gbk=gbk: v.tensor_single_scalar(
                        gmsk[:, sl, 0, :], gbk[:, 0:NEXP], gsc[:, sl, 0:1],
                        ALU.is_ge), deps=[q[("g1", t)]], aps=True)
                for t in batch:
                    sl = t % 4
                    gbk = [pg, pgt, py[0], py[1]][t % 4]
                    q[("ms", t)] = add("vector", lambda v, sl=sl, gbk=gbk: v.scalar_tensor_tensor(
                        gmsk[:, sl, 1, :], gmsk[:, sl, 0, :], -1e30,
                        gbk[:, 0:NEXP], ALU.mult, ALU.add), deps=[q[("k1", t)]])
                    gfree4[t % 4][0] = q[("ms", t)]
                for t in batch:
                    sl = t % 4
                    q[("g2", t)] = add("vector", lambda v, sl=sl: v.reduce_max(
                        gsc[:, sl, 1:2], gmsk[:, sl, 1, :], axis=AX.X),
                        deps=[q[("ms", t)]])
                for t in batch:
                    sl = t % 4
                    q[("k2", t)] = add("vector", lambda v, sl=sl: v.tensor_single_scalar(
                        gmsk[:, sl, 2, :], gmsk[:, sl, 1, :], gsc[:, sl, 1:2],
                        ALU.is_ge), deps=[q[("g2", t)]], aps=True)
                for t in batch:
                    sl = t % 4
                    q[("df", t)] = add("vector", lambda v, sl=sl: v.tensor_sub(
                        gsc[:, sl, 2:3], gsc[:, sl, 0:1], gsc[:, sl, 1:2]),
                        deps=[q[("g2", t)]])
                for t in batch:
                    sl = t % 4
                    q[("p1", t)] = add("scalar", lambda ss, sl=sl: ss.activation(
                        gsc[:, sl, 3:4], gsc[:, sl, 2:3], AF.Sigmoid),
                        deps=[q[("df", t)]])
                for t in batch:
                    sl = t % 4
                    q[("p2", t)] = add("vector", lambda v, sl=sl: v.tensor_scalar(
                        gsc[:, sl, 4:5], gsc[:, sl, 3:4], -1.0, 1.0,
                        ALU.mult, ALU.add), deps=[q[("p1", t)]])
                for t in batch:
                    sl = t % 4
                    q[("c1", t)] = add("vector", lambda v, t=t, sl=sl: v.tensor_scalar_mul(
                        c_sb[:, t, :], gmsk[:, sl, 0, :], gsc[:, sl, 3:4]),
                        deps=[q[("p2", t)]])
                for t in batch:
                    sl = t % 4
                    c_tok[t] = add("vector", lambda v, t=t, sl=sl: v.scalar_tensor_tensor(
                        c_sb[:, t, :], gmsk[:, sl, 2, :], gsc[:, sl, 4:5],
                        c_sb[:, t, :], ALU.mult, ALU.add),
                        deps=[q[("c1", t)], q[("k2", t)]], aps=True)

            pg_free = [gfree4[0][0]]
            pcT_free = [gfree4[0][0]]
            for t in range(NT):
                t_trc = add("tensor", lambda tn, t=t: tn.transpose(
                    pg[0:8, 0:128], c_sb[:, t, :], idn32[:, :]),
                    deps=[c_tok[t], pcT_free[0], t_idn32])
                t_evc = add("vector", lambda v, t=t: v.tensor_copy(
                    cT[:, 128 * t:128 * t + 128], pg[0:8, 0:128]), deps=[t_trc])
                pcT_free[0] = t_evc
                cT_toks.append(t_evc)
            pg_free[0] = pcT_free[0]

            if DEBUG:
                add("sync", lambda s: s.dma_start(dbg_c[:, :, :], c_sb[:, :, :]),
                    deps=[c_tok[t2] for t2 in range(NT)], dma=True)

            ltri = nc.alloc_sbuf_tensor_at(
                "ltri", [128, 128], BF16, offset=nc.lookup_mloc(sq2).addr + 1536)
            allon = nc.alloc_sbuf_tensor_at(
                "allon", [128, 128], BF16, offset=nc.lookup_mloc(sq2).addr + 1792)
            t_is = add("gpsimd", lambda g: g.iota(
                iotaS[:, :], [[1, CAP]], base=0, channel_multiplier=0,
                allow_small_or_imprecise_dtypes=True),
                deps=[P.last["scalar"]])
            t_lt = add("vector", lambda v: v.tensor_single_scalar(
                ltri[:, :], iot[:, :], 0.0, ALU.is_gt),
                deps=[t_iota, P.last["scalar"]])
            t_ao = add("vector", lambda v: v.memset(allon[:, :], 1.0),
                       deps=[P.last["scalar"]])

            gather_done = [None]
            scatter_done = [None]
            h_free = None
            ph_free = [None, None]
            pyp = [py[0], py[1], pg, pgt]
            py_free = [gfree4[2][0], gfree4[3][0], pg_free[0], gfree4[1][0]]
            pgt_free = [None]
            x_wtok = {}
            yacc_tok = {}
            lnTg_tok = {}
            cg_tok = {}
            prev_y16r = [None]
            rl_tok = {}
            out_toks = []

            for e in range(NEXP):
                t_mk = add("vector", lambda v, e=e: v.tensor_single_scalar(
                    msk16[:, :], c_sb[:, :, e], 0.0, ALU.is_gt),
                    deps=[c_tok[NT - 1]])
                t_um = add("vector", lambda v: v.tensor_scalar(
                    umask[:, :], msk16[:, :], -1.0, 1.0, ALU.mult, ALU.add),
                    deps=[t_mk])
                rmm = None
                first = True
                for th in range(2):
                    for tt in range(NTT):
                        t_abs = 8 * th + tt
                        last = (th == 1 and tt == NTT - 1)
                        for j in range(tt + 1):
                            lhs = ltri if j == tt else allon
                            rmm = add("tensor", lambda tn, lhs=lhs, t_abs=t_abs, th=th, j=j, first=first, stop=(last and j == tt): tn.matmul(
                                pg[:, t_abs:t_abs + 1], lhs[:, :],
                                msk16[:, 8 * th + j:8 * th + j + 1],
                                start=first, stop=stop),
                                deps=([t_mk, t_lt, t_ao, py_free[2]] if first else []))
                            first = False
                t_rk = add("vector", lambda v: v.scalar_tensor_tensor(
                    rank_sb[:, :], umask[:, :], BIG, pg[:, 0:NT],
                    ALU.mult, ALU.add), deps=[rmm, t_um])
                py_free[2] = t_rk

                pm_toks = []
                pmat_last = [None]
                for th in range(2):
                    for tt in range(NTT):
                        t_abs = 8 * th + tt
                        t_pm = add("vector", lambda v, th=th, tt=tt, t_abs=t_abs, e=e: v.tensor_scalar(
                            pmat[:, th, tt, :], iotaS[:, :],
                            rank_sb[:, t_abs:t_abs + 1],
                            c_sb[:, t_abs, e:e + 1],
                            ALU.is_equal, ALU.mult),
                            deps=[t_rk, t_is, gather_done[0], prev_y16r[0]],
                            aps=True)
                        pm_toks.append(t_pm)
                        pmat_last[0] = t_pm
                        pp0 = t_abs % 2
                        t_p0 = add("vector", lambda v, t_abs=t_abs, pp0=pp0: v.tensor_single_scalar(
                            p0scr[:, pp0, :], iotaS[:, :],
                            rank_sb[:, t_abs:t_abs + 1], ALU.is_equal),
                            deps=[t_rk, t_is], aps=True)
                        for scc in range(NSC):
                            pp = scc % 2
                            t_tp = add("tensor", lambda tn, pp0=pp0, scc=scc, pp=pp: tn.transpose(
                                ptr2[pp][:, :],
                                p0scr[:, pp0, 128 * scc:128 * scc + 128],
                                idn[:, :]),
                                deps=[t_p0, ptr2_free[pp], scatter_done[0]])
                            t_pe = add("vector", lambda v, th=th, scc=scc, tt=tt, pp=pp: v.tensor_copy(
                                p0s[:, th, scc, 128 * tt:128 * tt + 128],
                                ptr2[pp][:, :]), deps=[t_tp])
                            ptr2_free[pp] = t_pe
                            pm_toks.append(t_pe)

                gmm_last = None
                gidx = 0
                gfree = [py_free[3], py_free[2]]
                for th in range(2):
                    for dc in range(9):
                        gbk = [pgt, pg][gidx % 2]
                        gmm = None
                        for tt in range(NTT):
                            if dc < 8:
                                lh = (lambda tn, th=th, tt=tt, dc=dc: ln2N[:, 8 * th + tt, 128 * dc:128 * dc + 128])
                            else:
                                lh = (lambda tn: ones_c[:, 0:1])
                            dd3 = ([pmat_last[0], gfree[gidx % 2], t_oc,
                                    ln2N_toks[8 * th + tt]] if tt == 0 else [])
                            gmm = add("tensor", lambda tn, lh=lh, th=th, tt=tt, dc=dc, gbk=gbk: tn.matmul(
                                gbk[0:(128 if dc < 8 else 1), 0:CAP], lh(tn),
                                pmat[:, th, tt, :],
                                start=(tt == 0), stop=(tt == NTT - 1)), deps=dd3)
                        if dc < 8:
                            t_ge = add("vector", lambda v, th=th, dc=dc, gbk=gbk: v.tensor_copy(
                                lnTg[:, th, dc, :], gbk[:, 0:CAP]),
                                deps=[gmm, h_free, lnTg_tok.get((th, dc))])
                            lnTg_tok[(th, dc)] = t_ge
                        else:
                            t_ge = add("vector", lambda v, th=th, gbk=gbk: v.tensor_copy(
                                c_g[0:1, th, :], gbk[0:1, 0:CAP]),
                                deps=[gmm, h_free, cg_tok.get(th)])
                            cg_tok[th] = t_ge
                        gfree[gidx % 2] = t_ge
                        gidx += 1
                        gmm_last = gmm
                py_free[2], py_free[3] = gfree[1], gfree[0]
                gather_done[0] = gmm_last

                for mj in range(8):
                    m = 8 * e + mj
                    for half in range(2):
                        for th in range(2):
                            for fc in range(2):
                                pb = (2 * th + fc) % 2
                                hmm = None
                                for ci in range(8):
                                    dd4 = []
                                    if ci == 0:
                                        dd4 = [w1dma[2 * m + half], ph_free[pb],
                                               lnTg_tok[(th, 0)], lnTg_tok[(th, 7)]]
                                    hmm = add("tensor", lambda tn, th=th, fc=fc, ci=ci, half=half, pb=pb: tn.matmul(
                                        ph[pb][:, 0:CAP],
                                        w1_sb[:, half, ci, 128 * fc:128 * fc + 128],
                                        lnTg[:, th, ci, :],
                                        start=(ci == 0), stop=False), deps=dd4)
                                hmm = add("tensor", lambda tn, th=th, fc=fc, half=half, pb=pb: tn.matmul(
                                    ph[pb][:, 0:CAP],
                                    b1r[0:1, half, 128 * fc:128 * fc + 128],
                                    c_g[0:1, th, :],
                                    start=False, stop=True),
                                    deps=[cg_tok[th]])
                                slot = 2 * half + fc
                                t_rl = add("scalar", lambda s, th=th, slot=slot, pb=pb: s.activation(
                                    h_sb[:, th, slot, :],
                                    ph[pb][:, 0:CAP], AF.Relu),
                                    deps=[hmm, rl_tok.get((th, slot))])
                                ph_free[pb] = t_rl
                                h_free = t_rl
                                rl_tok[(th, slot)] = t_rl
                        plan_w1_dma(m + 1, half, [P.last["tensor"]])
                    yq = 0
                    for th in range(2):
                        for st in range(NSC):
                            for dh in range(2):
                                pp = yq % 4
                                yq += 1
                                ymm = None
                                for cj in range(4):
                                    dd5 = [py_free[pp], rl_tok[(th, 3)],
                                           rl_tok[(th, 2)]] if cj == 0 else []
                                    ymm = add("tensor", lambda tn, th=th, st=st, dh=dh, cj=cj, pp=pp, m=m: tn.matmul(
                                        pyp[pp][:, :],
                                        h_sb[:, th, cj, 128 * st:128 * st + 128],
                                        w2_sb[:, m % 2, cj, 512 * dh:512 * dh + 512],
                                        start=(cj == 0), stop=(cj == 3)),
                                        deps=dd5 + ([w2dma[m]] if cj == 0 else []))
                                if mj == 0:
                                    t_ya = add("vector", lambda v, th=th, st=st, dh=dh, pp=pp: v.tensor_copy(
                                        yacc[:, th, st, 512 * dh:512 * dh + 512],
                                        pyp[pp][:, :]),
                                        deps=[ymm, yacc_tok.get((th, st, dh))])
                                else:
                                    t_ya = add("vector", lambda v, th=th, st=st, dh=dh, pp=pp: v.tensor_add(
                                        yacc[:, th, st, 512 * dh:512 * dh + 512],
                                        yacc[:, th, st, 512 * dh:512 * dh + 512],
                                        pyp[pp][:, :]),
                                        deps=[ymm, yacc_tok.get((th, st, dh))])
                                yacc_tok[(th, st, dh)] = t_ya
                                py_free[pp] = t_ya
                    pair_done[m] = P.last["tensor"]
                    plan_w2_dma(m + 2, [pair_done[m]])

                yc_toks = []
                yc_th = {}
                for th in range(2):
                    for st in range(NSC):
                        for dh in range(2):
                            t_yc = add("vector", lambda v, th=th, st=st, dh=dh: v.tensor_copy(
                                y16[:, th, st, 512 * dh:512 * dh + 512],
                                yacc[:, th, st, 512 * dh:512 * dh + 512]),
                                deps=[yacc_tok[(th, st, dh)], gather_done[0]])
                            yc_toks.append(t_yc)
                    yc_th[th] = yc_toks[-1]
                smm_last = None
                sq_i = 0
                for th in range(2):
                    for tt in range(NTT):
                        t_abs = 8 * th + tt
                        for dh in range(2):
                            pp = sq_i % 4
                            sq_i += 1
                            smm = None
                            for scc in range(NSC):
                                dd6 = [py_free[pp], yc_th[th], pm_toks[-1]] if scc == 0 else []
                                smm = add("tensor", lambda tn, th=th, tt=tt, dh=dh, scc=scc, pp=pp: tn.matmul(
                                    pyp[pp][:, :],
                                    p0s[:, th, scc, 128 * tt:128 * tt + 128],
                                    y16[:, th, scc, 512 * dh:512 * dh + 512],
                                    start=(scc == 0), stop=(scc == NSC - 1)),
                                    deps=dd6)
                            smm_last = smm
                            t_xa = add("vector", lambda v, t_abs=t_abs, dh=dh, pp=pp: v.tensor_add(
                                x_sb[:, t_abs, 512 * dh:512 * dh + 512],
                                x_sb[:, t_abs, 512 * dh:512 * dh + 512],
                                pyp[pp][:, :]),
                                deps=[smm, x_wtok.get((t_abs, dh))])
                            x_wtok[(t_abs, dh)] = t_xa
                            py_free[pp] = t_xa
                        if e == NEXP - 1:
                            for dh in range(2):
                                pp = sq_i % 4
                                sq_i += 1
                                t_bm = add("tensor", lambda tn, t_abs=t_abs, dh=dh, pp=pp: tn.matmul(
                                    pyp[pp][:, :], cT[0:8, 128 * t_abs:128 * t_abs + 128],
                                    b2_sb[0:8, 512 * dh:512 * dh + 512],
                                    start=True, stop=True),
                                    deps=[cT_toks[t_abs], t_b2l, py_free[pp]])
                                t_fa = add("vector", lambda v, t_abs=t_abs, dh=dh, pp=pp: v.tensor_add(
                                    x_sb[:, t_abs, 512 * dh:512 * dh + 512],
                                    x_sb[:, t_abs, 512 * dh:512 * dh + 512],
                                    pyp[pp][:, :]), deps=[t_bm, x_wtok[(t_abs, dh)]])
                                py_free[pp] = t_fa
                                x_wtok[(t_abs, dh)] = t_fa
                            oeng = ("sync", "gpsimd")[t_abs % 2]
                            out_toks.append(add(oeng, lambda en, t_abs=t_abs: en.dma_start(
                                out[128 * t_abs:128 * t_abs + 128, :],
                                x_sb[:, t_abs, :]),
                                deps=[x_wtok[(t_abs, 0)], x_wtok[(t_abs, 1)]],
                                dma=True))
                scatter_done[0] = smm_last
                prev_y16r[0] = smm_last

            add("sync", lambda s: s.nop(), deps=out_toks)

        LAST_PLAN["n_ops"] = len(P.ops)
        LAST_PLAN["P"] = P
        P.prepare()
        with nc.Block() as block:

            @block.tensor
            def _(e):
                P.emit_engine("tensor", e, prog_sem, dma_sem)

            @block.vector
            def _(e):
                P.emit_engine("vector", e, prog_sem, dma_sem)

            @block.scalar
            def _(e):
                P.emit_engine("scalar", e, prog_sem, dma_sem)

            @block.gpsimd
            def _(e):
                P.emit_engine("gpsimd", e, prog_sem, dma_sem)

            @block.sync
            def _(e):
                P.emit_engine("sync", e, prog_sem, dma_sem)
    return nc


_CACHE = {}


def kernel(**inputs):
    if "nc" not in _CACHE:
        _CACHE["nc"] = build_program()
    nc = _CACHE["nc"]
    f32 = lambda k: np.ascontiguousarray(np.asarray(inputs[k], dtype=np.float32))
    x = f32("L0_attn_x")
    W = {
        "wq": f32("L0_attn_Wq"), "bq": f32("L0_attn_bq"),
        "wk": f32("L0_attn_Wk"), "bk": f32("L0_attn_bk"),
        "wv": f32("L0_attn_Wv"), "bv": f32("L0_attn_bv"),
        "wo": f32("L0_attn_Wo"), "bo": f32("L0_attn_bo"),
        "gw": f32("L0_ffn_gate_w"), "gb": f32("L0_ffn_gate_b"),
        "w1": f32("L0_ffn_w1"), "b1": f32("L0_ffn_b1"),
        "w2": f32("L0_ffn_w2"), "b2": f32("L0_ffn_b2"),
    }
    in_maps = []
    for c in range(8):
        b, h = divmod(c, 2)
        m = dict(W)
        m["x_own"] = np.ascontiguousarray(x[b, 2048 * h:2048 * h + 2048])
        m["x_oth"] = np.ascontiguousarray(x[b, 2048 * (1 - h):2048 * (1 - h) + 2048])
        in_maps.append(m)
    res = run_bass_kernel_spmd(nc, in_maps, core_ids=list(range(8)))
    outv = np.empty((4, 4096, 1024), np.float32)
    for c in range(8):
        b, h = divmod(c, 2)
        outv[b, 2048 * h:2048 * h + 2048] = res.results[c]["out"]
    return outv


# version 19
# speedup vs baseline: 1.0081x; 1.0081x over previous
import contextlib
import numpy as np

import concourse.bass as bass
import concourse.mybir as mybir
from concourse.bass_utils import run_bass_kernel_spmd

F32 = mybir.dt.float32
BF16 = mybir.dt.bfloat16
AF = mybir.ActivationFunctionType
ALU = mybir.AluOpType
AX = mybir.AxisListType

D = 1024
NHEAD = 16
NGRP = 4
HD = 64
NTOK = 2048
NT = 16
NKT = 32
DFF = 4096
NEXP = 8
QB = 256
NQB = NTOK // QB
PAIRS = [(0, 4), (1, 5), (2, 6), (3, 7), (8, 12), (9, 13), (10, 14), (11, 15)]
PAIR_OF = {}
for _pr, _hs in enumerate(PAIRS):
    PAIR_OF[_hs[0]] = (_pr, 0)
    PAIR_OF[_hs[1]] = (_pr, 1)

ENGS = ("tensor", "vector", "scalar", "gpsimd", "sync")
DEBUG = False
LAST_PLAN = {}
HARDEN = (0, 0)


class Planner:

    NLANE = 6

    def __init__(self):
        self.ops = []
        self.nseq = {e: 0 for e in ENGS}
        self.ndma = {e: 0 for e in ENGS}
        self.last = {e: None for e in ENGS}

    def add(self, eng, fn, deps=(), dma=False, tag="", aps=False):
        if HARDEN[1] > HARDEN[0] and HARDEN[0] <= len(self.ops) < HARDEN[1] and self.ops:
            deps = list(deps) + [self.ops[-1]]
        flat = []
        for d in deps:
            if d is None:
                continue
            if isinstance(d, (list, tuple)):
                flat.extend(x for x in d if x is not None)
            else:
                flat.append(d)
        rec = {"eng": eng, "fn": fn, "deps": flat, "dma": dma,
               "seq": self.nseq[eng], "need": False, "tag": tag,
               "dmaseq": None, "aps": aps}
        self.nseq[eng] += 1
        if dma:
            lane = self.ndma[eng] % self.NLANE
            self.ndma[eng] += 1
            rec["lane"] = lane
            rec["laneuse"] = (self.ndma[eng] - 1) // self.NLANE + 1
            rec["dmaseq"] = self.ndma[eng]
        self.ops.append(rec)
        best = {}
        red = []
        for d in flat:
            if d["dma"]:
                red.append(d)
            elif d["eng"] != eng:
                k = d["eng"]
                if k not in best or d["seq"] > best[k]["seq"]:
                    best[k] = d
        for d in best.values():
            d["need"] = True
            red.append(d)
        rec["rdeps"] = red
        self.last[eng] = rec
        return rec

    def barrier_tokens(self):
        return [t for t in self.last.values() if t is not None]

    def prepare(self):
        self.inc_id = {}
        cnt = {e: 0 for e in ENGS}
        for rec in self.ops:
            if rec["need"]:
                cnt[rec["eng"]] += 1
            self.inc_id[id(rec)] = cnt[rec["eng"]]

    def emit_engine(self, ename, eng, prog_sem, dma_sem):
        DRAIN_DIST = 2
        APS_DIST = 8
        last_wait = {}
        drained_at = -1
        for rec in self.ops:
            if rec["eng"] != ename:
                continue
            need_drain = False
            lim = APS_DIST if rec["aps"] else DRAIN_DIST
            for d in rec["deps"]:
                if not d["dma"] and d["eng"] == ename:
                    if (d["seq"] > drained_at
                            and rec["seq"] - d["seq"] <= lim):
                        need_drain = True
            for d in rec["rdeps"]:
                if d["dma"]:
                    sem, val = dma_sem[d["eng"]][d["lane"]], 16 * d["laneuse"]
                else:
                    sem, val = prog_sem[d["eng"]], self.inc_id[id(d)]
                if last_wait.get(id(sem), -1) >= val:
                    continue
                eng.wait_ge(sem, val)
                last_wait[id(sem)] = val
            if need_drain:
                eng.drain()
                drained_at = rec["seq"] - 1
            if rec["dma"]:
                lsem = dma_sem[ename][rec["lane"]]
                lval = 16 * (rec["laneuse"] - 1)
                if lval > 0 and last_wait.get(id(lsem), -1) < lval:
                    eng.wait_ge(lsem, lval)
                    last_wait[id(lsem)] = lval
                inst = rec["fn"](eng)
                inst.then_inc(lsem, 16)
            else:
                inst = rec["fn"](eng)
                if rec["need"]:
                    inst.then_inc(prog_sem[ename], 1)


def build_program():
    nc = bass.Bass("TRN2", target_bir_lowering=False, detect_race_conditions=False)
    P = Planner()
    add = P.add

    xo = nc.dram_tensor("x_own", [NTOK, D], F32, kind="ExternalInput").ap()
    xr = nc.dram_tensor("x_oth", [NTOK, D], F32, kind="ExternalInput").ap()
    wq = nc.dram_tensor("wq", [D, D], F32, kind="ExternalInput").ap()
    bq = nc.dram_tensor("bq", [D], F32, kind="ExternalInput").ap()
    wk = nc.dram_tensor("wk", [D, 256], F32, kind="ExternalInput").ap()
    bk = nc.dram_tensor("bk", [256], F32, kind="ExternalInput").ap()
    wv = nc.dram_tensor("wv", [D, 256], F32, kind="ExternalInput").ap()
    bv = nc.dram_tensor("bv", [256], F32, kind="ExternalInput").ap()
    wo = nc.dram_tensor("wo", [D, D], F32, kind="ExternalInput").ap()
    bo = nc.dram_tensor("bo", [D], F32, kind="ExternalInput").ap()
    gw = nc.dram_tensor("gw", [D, NEXP], F32, kind="ExternalInput").ap()
    gb = nc.dram_tensor("gb", [NEXP], F32, kind="ExternalInput").ap()
    w1 = nc.dram_tensor("w1", [NEXP, D, DFF], F32, kind="ExternalInput").ap()
    b1 = nc.dram_tensor("b1", [NEXP, DFF], F32, kind="ExternalInput").ap()
    w2 = nc.dram_tensor("w2", [NEXP, DFF, D], F32, kind="ExternalInput").ap()
    b2 = nc.dram_tensor("b2", [NEXP, D], F32, kind="ExternalInput").ap()
    out = nc.dram_tensor("out", [NTOK, D], F32, kind="ExternalOutput").ap()
    if DEBUG:
        dbg_x1 = nc.dram_tensor("dbg_x1", [128, NT, D], F32, kind="ExternalOutput").ap()
        dbg_c = nc.dram_tensor("dbg_c", [128, NT, NEXP], F32, kind="ExternalOutput").ap()
        dbg_q = nc.dram_tensor("dbg_q", [128, 8, NTOK], F32, kind="ExternalOutput").ap()
        dbg_k = nc.dram_tensor("dbg_k", [128, NGRP, 128 * NKT], F32, kind="ExternalOutput").ap()
        dbg_v = nc.dram_tensor("dbg_v", [128, NGRP, NKT * 65], F32, kind="ExternalOutput").ap()
        dbg_bias = nc.dram_tensor("dbg_bias", [128, NHEAD], F32, kind="ExternalOutput").ap()

    with contextlib.ExitStack() as ctx:
        E = ctx.enter_context
        x_sb = E(nc.sbuf_tensor("x_sb", [128, NT, D], F32))
        bo_bc = E(nc.sbuf_tensor("bo_bc", [128, D], BF16))
        cT = E(nc.sbuf_tensor("cT", [8, NTOK], BF16))
        c_sb = E(nc.sbuf_tensor("c_sb", [128, NT, NEXP], F32))
        sums = E(nc.sbuf_tensor("sums", [128, 2 * NT], F32))
        sumsq = E(nc.sbuf_tensor("sumsq", [128, 2 * NT], F32))
        mean = E(nc.sbuf_tensor("mean", [128, 2 * NT], F32))
        rstd = E(nc.sbuf_tensor("rstd", [128, 2 * NT], F32))
        nmr = E(nc.sbuf_tensor("nmr", [128, 2 * NT], F32))
        var_ = E(nc.sbuf_tensor("var_", [128, 2 * NT], F32))
        bias_h = E(nc.sbuf_tensor("bias_h", [128, NHEAD], F32))
        bias_p = E(nc.sbuf_tensor("bias_p", [128, 8], F32))
        bq_sb = E(nc.sbuf_tensor("bq_sb", [128, 8], F32))
        bk_sb = E(nc.sbuf_tensor("bk_sb", [128, 2], F32))
        idn = E(nc.sbuf_tensor("idn", [128, 128], BF16))
        idn32 = E(nc.sbuf_tensor("idn32", [128, 128], F32))
        iot = E(nc.sbuf_tensor("iot", [128, 128], F32))
        ones = E(nc.sbuf_tensor("ones", [1, 128], BF16))
        gw_sb = E(nc.sbuf_tensor("gw_sb", [128, 8, NEXP], BF16))
        gb_sb = E(nc.sbuf_tensor("gb_sb", [1, NEXP], BF16))
        bv_sb = E(nc.sbuf_tensor("bv_sb", [1, 256], BF16))
        bo_sb = E(nc.sbuf_tensor("bo_sb", [1, D], BF16))
        b2_sb = E(nc.sbuf_tensor("b2_sb", [NEXP, D], BF16))
        qmaxsq = E(nc.sbuf_tensor("qmaxsq", [2, 4], F32))
        qmx = E(nc.sbuf_tensor("qmx", [2, 1], F32))
        kmaxsq = E(nc.sbuf_tensor("kmaxsq", [1, 8], F32))
        kg2 = E(nc.sbuf_tensor("kg2", [2, NGRP], F32))
        mneg = E(nc.sbuf_tensor("mneg", [2, 1], BF16))
        halfone = E(nc.sbuf_tensor("halfone", [128, 2], BF16))
        eps_sb = E(nc.sbuf_tensor("eps_sb", [128, 1], F32))
        sel2 = E(nc.sbuf_tensor("sel2", [2, 2, 128], BF16))
        iotp2 = E(nc.sbuf_tensor("iotp2", [2, 128], F32))
        kg16 = E(nc.sbuf_tensor("kg16", [1, NGRP], BF16))
        gsc = E(nc.sbuf_tensor("gsc", [128, 4, 6], F32))
        gmsk = E(nc.sbuf_tensor("gmsk", [128, 4, 3, NEXP], F32))

        prog_sem = {e: E(nc.semaphore(f"pg_{e}")) for e in ENGS}
        dma_sem = {e: [E(nc.semaphore(f"dm_{e}{i}")) for i in range(Planner.NLANE)]
                   for e in ENGS}

        t_iota = add("gpsimd", lambda g: g.iota(
            iot[:, :], [[1, 128]], base=0, channel_multiplier=-1,
            allow_small_or_imprecise_dtypes=True))
        t_idn = add("vector", lambda v: v.tensor_single_scalar(
            idn[:, :], iot[:, :], 0.0, ALU.is_equal), deps=[t_iota])
        t_idn32 = add("vector", lambda v: v.tensor_single_scalar(
            idn32[:, :], iot[:, :], 0.0, ALU.is_equal), deps=[t_iota])
        t_ones = add("vector", lambda v: v.memset(ones[:, :], 1.0))
        t_eps = add("vector", lambda v: v.memset(eps_sb[:, :], 1e-5))
        t_h0 = add("vector", lambda v: v.memset(halfone[:, :], 0.0))
        t_h1 = add("vector", lambda v: v.memset(halfone[0:64, 0:1], 1.0), deps=[t_h0])
        t_h2 = add("vector", lambda v: v.memset(halfone[64:128, 1:2], 1.0), deps=[t_h0])
        t_iotp = add("gpsimd", lambda g: g.iota(
            iotp2[:, :], [[0, 128]], base=0, channel_multiplier=1,
            allow_small_or_imprecise_dtypes=True))
        t_s1 = add("vector", lambda v: v.tensor_single_scalar(
            sel2[:, 0, :], iotp2[:, :], 0.0, ALU.is_equal), deps=[t_iotp])
        t_s2 = add("vector", lambda v: v.tensor_single_scalar(
            sel2[:, 1, :], iotp2[:, :], 1.0, ALU.is_equal), deps=[t_iotp])

        t_xown = add("sync", lambda s: s.dma_start(
            x_sb[:, :, :], xo.rearrange("(t p) j -> p t j", p=128)), dma=True)


        with contextlib.ExitStack() as actx:
            E = actx.enter_context
            kT = E(nc.sbuf_tensor("kT", [128, NGRP, 128 * NKT], BF16))
            v_sb = E(nc.sbuf_tensor("v_sb", [128, NGRP, NKT * 65], BF16))
            qT = E(nc.sbuf_tensor("qT", [128, 8, NTOK], BF16))
            t_vones = add("vector", lambda v: v.memset(v_sb[:, :, :], 1.0))

            bias_tok = {}

            with contextlib.ExitStack() as bctx:
                Eb = bctx.enter_context
                wq_sb = Eb(nc.sbuf_tensor("wq_sb", [128, 8, D], BF16))
                wk_sb = Eb(nc.sbuf_tensor("wk_sb", [128, 8, 256], BF16))
                wv_sb = Eb(nc.sbuf_tensor("wv_sb", [128, 8, 256], BF16))
                x_oth = Eb(nc.sbuf_tensor("xoth_sb", [128, 2, D], F32))
                ln_scr = Eb(nc.sbuf_tensor("ln_scr", [128, 6, D], BF16))
                lnT_scr = Eb(nc.sbuf_tensor("lnT_scr", [128, 2, 8, 128], BF16))
                normsq = nc.alloc_sbuf_tensor_at(
                    "normsq", [128, NTOK], BF16, offset=nc.lookup_mloc(ln_scr).addr)
                bqt = nc.alloc_sbuf_tensor_at(
                    "bqt", [8, 128], F32, offset=nc.lookup_mloc(ln_scr).addr + 2048)
                bkt = nc.alloc_sbuf_tensor_at(
                    "bkt", [2, 128], F32, offset=nc.lookup_mloc(ln_scr).addr + 2560)
                ptr = [Eb(nc.psum_tensor(f"ptr{i}", [128, 128], BF16)) for i in range(2)]
                pq = [Eb(nc.psum_tensor(f"pq{i}", [128, 128], F32)) for i in range(2)]
                pk = Eb(nc.psum_tensor("pk", [128, 256], F32))
                pvp = Eb(nc.psum_tensor("pvp", [128, 256], F32))
                pnb = Eb(nc.psum_tensor("pnb", [128, 512], F32))

                t_bqt = add("sync", lambda s: s.dma_start(
                    bqt[:, :], bq.rearrange("(c p) -> c p", p=128)), dma=True)
                t_bkt = add("sync", lambda s: s.dma_start(
                    bkt[:, :], bk.rearrange("(c p) -> c p", p=128)), dma=True)
                t_trbq = add("tensor", lambda tn: tn.transpose(
                    pq[0][:, 0:8], bqt[:, :], idn32[0:8, 0:8]),
                    deps=[t_bqt, t_idn32])
                t_bq = add("vector", lambda v: v.tensor_copy(
                    bq_sb[:, :], pq[0][:, 0:8]), deps=[t_trbq])
                t_trbk = add("tensor", lambda tn: tn.transpose(
                    pq[1][:, 0:2], bkt[:, :], idn32[0:2, 0:2]),
                    deps=[t_bkt, t_idn32])
                t_bk = add("vector", lambda v: v.tensor_copy(
                    bk_sb[:, :], pq[1][:, 0:2]), deps=[t_trbk])
                t_wq = add("gpsimd", lambda g: g.dma_start(
                    wq_sb[:, :, :], wq.rearrange("(c p) j -> p c j", p=128)), dma=True)
                t_wk = add("gpsimd", lambda g: g.dma_start(
                    wk_sb[:, :, :], wk.rearrange("(c p) j -> p c j", p=128)), dma=True)
                t_wv = add("gpsimd", lambda g: g.dma_start(
                    wv_sb[:, :, :], wv.rearrange("(c p) j -> p c j", p=128)), dma=True)
                t_bv = add("gpsimd", lambda g: g.dma_start(
                    bv_sb[0:1, :], bv[None, :]), dma=True)
                t_gw = add("gpsimd", lambda g: g.dma_start(
                    gw_sb[:, :, :], gw.rearrange("(c p) j -> p c j", p=128)), dma=True)
                t_gb = add("gpsimd", lambda g: g.dma_start(
                    gb_sb[0:1, :], gb[None, :]), dma=True)
                t_bo = add("gpsimd", lambda g: g.dma_start(
                    bo_sb[0:1, :], bo[None, :]), dma=True)
                t_b2l = add("gpsimd", lambda g: g.dma_start(
                    b2_sb[:, :], b2[:, :]), dma=True)

                xoth_free = [None, None]
                lnT_free = [None, None]
                lnscr_free = [None] * 6
                ptr_free = [None, None]
                pq_free = [t_bq, t_bk]
                pk_free = [None]
                pv_free = [None]

                ORDER = []
                for k2 in range(NT):
                    ORDER.append((True, k2))
                    ORDER.append((False, k2))

                def srcs_of(gi):
                    own, t = ORDER[gi]
                    if own:
                        return x_sb[:, t, :]
                    return x_oth[:, t % 2, :]

                ctoks = {}

                def plan_chain(g0, ph):
                    batch = list(range(g0, g0 + 4))
                    if ph == 0:
                        for gi in batch:
                            own, t = ORDER[gi]
                            if not own:
                                ctoks[("src", gi)] = add(
                                    "sync", lambda sy, t=t: sy.dma_start(
                                        x_oth[:, t % 2, :],
                                        xr[128 * t:128 * t + 128, :]),
                                    deps=[xoth_free[t % 2]], dma=True)
                            else:
                                ctoks[("src", gi)] = t_xown
                        for gi in batch:
                            sc = (slice(None), slice(gi, gi + 1))
                            ctoks[("rs", gi)] = add("vector", lambda v, gi=gi, sc=sc: v.reduce_sum(
                                sums[sc], srcs_of(gi), axis=AX.X), deps=[ctoks[("src", gi)]])
                        for gi in batch:
                            sc = (slice(None), slice(gi, gi + 1))
                            slot = gi % 6
                            ctoks[("sq", gi)] = add("scalar", lambda ss, gi=gi, sc=sc, slot=slot: ss.activation(
                                ln_scr[:, slot, :], srcs_of(gi), AF.Square,
                                accum_out=sumsq[sc]),
                                deps=[ctoks[("src", gi)], lnscr_free[slot],
                                      t_bq, t_bk])
                    elif ph == 1:
                        for gi in batch:
                            sc = (slice(None), slice(gi, gi + 1))
                            ctoks[("m", gi)] = add("vector", lambda v, sc=sc: v.tensor_single_scalar(
                                mean[sc], sums[sc], 1.0 / D, ALU.mult), deps=[ctoks[("rs", gi)]])
                        for gi in batch:
                            sc = (slice(None), slice(gi, gi + 1))
                            ctoks[("s2", gi)] = add("vector", lambda v, sc=sc: v.tensor_single_scalar(
                                var_[sc], sumsq[sc], 1.0 / D, ALU.mult), deps=[ctoks[("sq", gi)]])
                        for gi in batch:
                            sc = (slice(None), slice(gi, gi + 1))
                            ctoks[("mm", gi)] = add("vector", lambda v, sc=sc: v.tensor_mul(
                                nmr[sc], mean[sc], mean[sc]), deps=[ctoks[("m", gi)]])
                        for gi in batch:
                            sc = (slice(None), slice(gi, gi + 1))
                            ctoks[("var", gi)] = add("vector", lambda v, sc=sc: v.tensor_sub(
                                var_[sc], var_[sc], nmr[sc]),
                                deps=[ctoks[("s2", gi)], ctoks[("mm", gi)]])
                    elif ph == 2:
                        for gi in batch:
                            sc = (slice(None), slice(gi, gi + 1))
                            ctoks[("std", gi)] = add("scalar", lambda ss, sc=sc: ss.activation(
                                rstd[sc], var_[sc], AF.Sqrt, bias=eps_sb[:, 0:1]),
                                deps=[ctoks[("var", gi)], t_eps])
                        for gi in batch:
                            sc = (slice(None), slice(gi, gi + 1))
                            ctoks[("rst", gi)] = add("vector", lambda v, sc=sc: v.reciprocal(
                                rstd[sc], rstd[sc]), deps=[ctoks[("std", gi)]])
                        for gi in batch:
                            sc = (slice(None), slice(gi, gi + 1))
                            ctoks[("nmr", gi)] = add("vector", lambda v, sc=sc: v.scalar_tensor_tensor(
                                nmr[sc], mean[sc], -1.0, rstd[sc], ALU.mult, ALU.mult),
                                deps=[ctoks[("rst", gi)]])
                    else:
                        for gi in batch:
                            sc = (slice(None), slice(gi, gi + 1))
                            slot = gi % 6
                            ctoks[("ln", gi)] = add("scalar", lambda ss, gi=gi, sc=sc, slot=slot: ss.activation(
                                ln_scr[:, slot, :], srcs_of(gi), AF.Identity,
                                bias=nmr[sc], scale=rstd[sc]),
                                deps=[ctoks[("nmr", gi)], lnscr_free[slot]])
                            own, t = ORDER[gi]
                            if not own:
                                xoth_free[t % 2] = [ctoks[("ln", gi)], ctoks[("rs", gi)],
                                                    ctoks[("sq", gi)]]

                for ph in range(4):
                    plan_chain(0, ph)

                for g0 in range(0, 2 * NT, 4):
                    batch = list(range(g0, g0 + 4))
                    nxt = g0 + 4
                    for ti, gi in enumerate(batch):
                        own, t = ORDER[gi]
                        gt = t if own else NT + t
                        par = gi % 6
                        t_ln = ctoks[("ln", gi)]
                        ev_toks = []
                        t_tr = None
                        for c in range(8):
                            pp = c % 2
                            t_tr = add("tensor", lambda tn, c=c, par=par, pp=pp: tn.transpose(
                                ptr[pp][:, :], ln_scr[:, par, 128 * c:128 * c + 128],
                                idn[:, :]),
                                deps=[t_ln, t_idn, ptr_free[pp], lnT_free[gi % 2]])
                            t_ev = add("vector", lambda v, c=c, gi=gi, pp=pp: v.tensor_copy(
                                lnT_scr[:, gi % 2, c, :], ptr[pp][:, :]), deps=[t_tr])
                            ptr_free[pp] = t_ev
                            ev_toks.append(t_ev)
                        lnscr_free[par] = t_tr
                        lnT = lnT_scr[:, gi % 2]

                        kmm = None
                        for kc in range(2):
                            for ci in range(8):
                                dd = [t_wk, pk_free[0]] + ev_toks if (kc == 0 and ci == 0) else [ev_toks[ci]]
                                kmm = add("tensor", lambda tn, kc=kc, ci=ci, lnT=lnT: tn.matmul(
                                    pk[:, 128 * kc:128 * kc + 128],
                                    wk_sb[:, ci, 128 * kc:128 * kc + 128],
                                    lnT[:, ci, :],
                                    start=(kc == 0 and ci == 0),
                                    stop=(kc == 1 and ci == 7)), deps=dd)
                        k_ev = None
                        for g in range(NGRP):
                            kc, hh = g // 2, g % 2
                            for dup in range(2):
                                k_ev = add("scalar", lambda ss, g=g, kc=kc, hh=hh, dup=dup, gt=gt: ss.activation(
                                    kT[64 * dup:64 * dup + 64, g, 128 * gt:128 * gt + 128],
                                    pk[64 * hh:64 * hh + 64, 128 * kc:128 * kc + 128],
                                    AF.Identity, bias=bk_sb[64 * hh:64 * hh + 64, kc:kc + 1]),
                                    deps=[kmm, t_bk])
                        pk_free[0] = k_ev

                        vmm = None
                        for ci in range(8):
                            dd = [t_wv, pv_free[0], ev_toks[0]] if ci == 0 else [ev_toks[ci]]
                            vmm = add("tensor", lambda tn, ci=ci, lnT=lnT: tn.matmul(
                                pvp[:, :], lnT[:, ci, :], wv_sb[:, ci, :],
                                start=(ci == 0), stop=False), deps=dd)
                        vmm = add("tensor", lambda tn: tn.matmul(
                            pvp[:, :], ones[0:1, :], bv_sb[0:1, :],
                            start=False, stop=True), deps=[t_bv, t_ones])
                        v_ev = None
                        for g in range(NGRP):
                            v_ev = add("vector", lambda v, g=g, gt=gt: v.tensor_copy(
                                v_sb[:, g, 65 * gt:65 * gt + 64],
                                pvp[:, 64 * g:64 * g + 64]), deps=[vmm, t_vones])
                        pv_free[0] = v_ev
                        last_cons = v_ev

                        if own:
                            for j in range(8):
                                pp = j % 2
                                qmm = None
                                for ci in range(8):
                                    dd = [t_wq, pq_free[pp], ev_toks[0]] if ci == 0 else [ev_toks[ci]]
                                    qmm = add("tensor", lambda tn, j=j, ci=ci, pp=pp, lnT=lnT: tn.matmul(
                                        pq[pp][:, :], wq_sb[:, ci, 128 * j:128 * j + 128],
                                        lnT[:, ci, :], start=(ci == 0), stop=(ci == 7)),
                                        deps=dd)
                                q_ev = None
                                for hh in range(2):
                                    h = 2 * j + hh
                                    pr, half = PAIR_OF[h]
                                    q_ev = add("scalar", lambda ss, hh=hh, pr=pr, half=half, t=t, pp=pp, j=j: ss.activation(
                                        qT[64 * half:64 * half + 64, pr, 128 * t:128 * t + 128],
                                        pq[pp][64 * hh:64 * hh + 64, :], AF.Identity,
                                        bias=bq_sb[64 * hh:64 * hh + 64, j:j + 1]),
                                        deps=[qmm, t_bq])
                                pq_free[pp] = q_ev
                                last_cons = q_ev
                        lnT_free[gi % 2] = last_cons
                        if nxt < 2 * NT:
                            plan_chain(nxt, ti)

                b_exit_b1 = P.barrier_tokens()

                pnb_free = [None]
                normsq_free = [None]
                for g in range(NGRP):
                    mx_toks = []
                    for hr in range(2):
                        t_ksq = add("scalar", lambda s, g=g, hr=hr: s.activation(
                            normsq[0:64, :], kT[0:64, g, 2048 * hr:2048 * hr + 2048],
                            AF.Square), deps=b_exit_b1 + [normsq_free[0]])
                        for i in range(4):
                            t_nm = add("tensor", lambda tn, i=i: tn.matmul(
                                pnb[0:1, :], halfone[0:64, 0:1],
                                normsq[0:64, 512 * i:512 * i + 512],
                                start=True, stop=True),
                                deps=[t_ksq, t_h1, pnb_free[0]] + (b_exit_b1 if (g == 0 and hr == 0 and i == 0) else []))
                            normsq_free[0] = t_nm
                            t_mx = add("vector", lambda v, i=i, hr=hr: v.reduce_max(
                                kmaxsq[0:1, 4 * hr + i:4 * hr + i + 1], pnb[0:1, :],
                                axis=AX.X), deps=[t_nm])
                            pnb_free[0] = t_mx
                            mx_toks.append(t_mx)
                    t_km = add("vector", lambda v, g=g: v.reduce_max(
                        kg2[0:1, g:g + 1], kmaxsq[0:1, :], axis=AX.X), deps=mx_toks)
                    t_ks = add("scalar", lambda s, g=g: s.activation(
                        kg2[0:1, g:g + 1], kg2[0:1, g:g + 1], AF.Sqrt), deps=[t_km])
                    t_k16 = add("vector", lambda v, g=g: v.tensor_copy(
                        kg16[0:1, g:g + 1], kg2[0:1, g:g + 1]), deps=[t_ks])
                    t_kbc = add("tensor", lambda tn, g=g: tn.matmul(
                        pnb[0:2, 0:1], ones[0:1, 0:2], kg16[0:1, g:g + 1],
                        start=True, stop=True),
                        deps=[t_k16, pnb_free[0], t_ones])
                    t_kcp = add("vector", lambda v, g=g: v.tensor_copy(
                        kg2[0:2, g:g + 1], pnb[0:2, 0:1]), deps=[t_kbc])
                    pnb_free[0] = t_kcp
                kg_done = P.last["vector"]

                mneg_free = [None]
                for pr in range(8):
                    g = PAIRS[pr][0] % NGRP
                    t_qsq = add("scalar", lambda s, pr=pr: s.activation(
                        normsq[:, 0:NTOK], qT[:, pr, :], AF.Square),
                        deps=[normsq_free[0]])
                    qm_toks = []
                    for c in range(4):
                        t_nm = add("tensor", lambda tn, c=c: tn.matmul(
                            pnb[0:2, :], halfone[:, :],
                            normsq[:, 512 * c:512 * c + 512],
                            start=True, stop=True),
                            deps=[t_qsq, t_h1, t_h2, pnb_free[0]])
                        normsq_free[0] = t_nm
                        t_mx = add("vector", lambda v, c=c: v.reduce_max(
                            qmaxsq[0:2, c:c + 1], pnb[0:2, :], axis=AX.X),
                            deps=[t_nm])
                        pnb_free[0] = t_mx
                        qm_toks.append(t_mx)
                    t_qm = add("vector", lambda v: v.reduce_max(
                        qmx[0:2, 0:1], qmaxsq[0:2, :], axis=AX.X), deps=qm_toks)
                    t_qs = add("scalar", lambda s: s.activation(
                        qmx[0:2, 0:1], qmx[0:2, 0:1], AF.Sqrt), deps=[t_qm])
                    t_mn = add("vector", lambda v, g=g: v.scalar_tensor_tensor(
                        mneg[0:2, 0:1], qmx[0:2, 0:1], -0.125, kg2[0:2, g:g + 1],
                        ALU.mult, ALU.mult), deps=[t_qs, kg_done, mneg_free[0]])
                    for half in range(2):
                        h = PAIRS[pr][half]
                        t_bc = add("tensor", lambda tn, half=half: tn.matmul(
                            pnb[:, 0:1], sel2[:, half, :], mneg[0:2, 0:1],
                            start=True, stop=True),
                            deps=[t_mn, pnb_free[0], t_s1, t_s2])
                        mneg_free[0] = t_bc
                        t_cp = add("vector", lambda v, h=h: v.tensor_copy(
                            bias_h[:, h:h + 1], pnb[:, 0:1]), deps=[t_bc])
                        pnb_free[0] = t_cp
                        bias_tok[h] = t_cp
                bias_ptok = {}
                for pr2 in range(8):
                    hA2, hB2 = PAIRS[pr2]
                    bias_ptok[pr2] = add("vector", lambda v, pr2=pr2, hA2=hA2, hB2=hB2: v.tensor_tensor(
                        bias_p[:, pr2:pr2 + 1], bias_h[:, hA2:hA2 + 1],
                        bias_h[:, hB2:hB2 + 1], ALU.min),
                        deps=[bias_tok[hA2], bias_tok[hB2]])

                bo_ready = []
                for dh in range(2):
                    t_bm0 = add("tensor", lambda tn, dh=dh: tn.matmul(
                        pnb[:, :], ones[0:1, :],
                        bo_sb[0:1, 512 * dh:512 * dh + 512],
                        start=True, stop=True),
                        deps=[t_bo, t_ones, pnb_free[0]])
                    t_cp0 = add("vector", lambda v, dh=dh: v.tensor_copy(
                        bo_bc[:, 512 * dh:512 * dh + 512], pnb[:, :]),
                        deps=[t_bm0])
                    pnb_free[0] = t_cp0
                    bo_ready.append(t_cp0)

                b_exit = P.barrier_tokens()

            with contextlib.ExitStack() as cctx:
                Ec = cctx.enter_context
                wo_sb = Ec(nc.sbuf_tensor("wo_sb", [128, 2, D], BF16))
                rb_sb = Ec(nc.sbuf_tensor("rb_sb", [64, 512], F32))
                recip16 = Ec(nc.sbuf_tensor("recip16", [1, 512], BF16))
                attn_pr = Ec(nc.sbuf_tensor("attn_pr", [128, QB], BF16))
                p_sb = Ec(nc.sbuf_tensor("p_sb", [128, 16 * 1024], BF16))
                sq2a = Ec(nc.sbuf_tensor("sq2a", [128, D], BF16))
                sAB = [Ec(nc.psum_tensor(f"sAB{i}", [128, 1024], F32)) for i in range(2)]
                pvb = [Ec(nc.psum_tensor(f"pvb{i}", [128, 512], F32)) for i in range(2)]
                ob = [Ec(nc.psum_tensor(f"ob{i}", [128, 512], F32)) for i in range(2)]

                wo_tok = [None, None]
                wo_free = [None, None]
                sc_free = {}
                pv_slot_prev = {}
                norm_done = [None, None]
                ob_free = [None, None]
                oadd_tok = {}
                it = 0
                last_oadd = None

                def plan_norm(cx):
                    par, wpar, qb = cx["par"], cx["wpar"], cx["qb"]
                    pv_mm = cx["pv_mm"]
                    t_rc = add("vector", lambda v, par=par: v.reciprocal(
                        rb_sb[0:1, 0:512], pvb[par][64:65, :]), deps=[pv_mm])
                    t_r16 = add("vector", lambda v: v.tensor_copy(
                        recip16[0:1, :], rb_sb[0:1, 0:512]), deps=[t_rc])
                    at_toks = []
                    for half in range(2):
                        t_bc = add("tensor", lambda tn, half=half: tn.matmul(
                            ob[half][0:64, 0:256], ones[0:1, 0:64],
                            recip16[0:1, 256 * half:256 * half + 256],
                            start=True, stop=True),
                            deps=[t_r16, ob_free[half]])
                        t_rb = add("vector", lambda v, half=half: v.tensor_copy(
                            rb_sb[0:64, 256 * half:256 * half + 256],
                            ob[half][0:64, 0:256]), deps=[t_bc])
                        ob_free[half] = t_rb
                        t_at = add("vector", lambda v, half=half, par=par: v.tensor_mul(
                            attn_pr[64 * half:64 * half + 64, :],
                            pvb[par][0:64, 256 * half:256 * half + 256],
                            rb_sb[0:64, 256 * half:256 * half + 256]),
                            deps=[t_rb])
                        at_toks.append(t_at)
                    norm_done[par] = at_toks[-1]
                    for tt in range(2):
                        xt = 2 * qb + tt
                        for dh in range(2):
                            op = (2 * tt + dh) % 2
                            t_om = add("tensor", lambda tn, tt=tt, dh=dh, op=op, wpar=wpar: tn.matmul(
                                ob[op][:, :], attn_pr[:, 128 * tt:128 * tt + 128],
                                wo_sb[:, wpar, 512 * dh:512 * dh + 512],
                                start=True, stop=True),
                                deps=at_toks + [wo_tok[wpar], ob_free[op]])
                            t_oa = add("vector", lambda v, xt=xt, dh=dh, op=op: v.tensor_add(
                                x_sb[:, xt, 512 * dh:512 * dh + 512],
                                x_sb[:, xt, 512 * dh:512 * dh + 512],
                                ob[op][:, :]), deps=[t_om, oadd_tok.get((xt, dh))])
                            ob_free[op] = t_oa
                            oadd_tok[(xt, dh)] = t_oa
                    wo_free[wpar] = P.last["tensor"]

                ln2tk = {}
                bo_add_tok = {}

                def plan_x1_final(k):
                    A = [2 * k, 2 * k + 1] if 0 <= k <= 7 else []
                    B = [2 * (k - 1), 2 * (k - 1) + 1] if 0 <= k - 1 <= 7 else []
                    C = [2 * (k - 2), 2 * (k - 2) + 1] if 0 <= k - 2 <= 7 else []
                    Dd = [2 * (k - 3), 2 * (k - 3) + 1] if 0 <= k - 3 <= 7 else []
                    for t in A:
                        bo_add_tok[t] = add("vector", lambda v, t=t: v.tensor_add(
                            x_sb[:, t, :], x_sb[:, t, :], bo_bc[:, :]),
                            deps=bo_ready + [oadd_tok[(t, 0)], oadd_tok[(t, 1)]])
                    for t in C:
                        ln2tk[("m", t)] = add("vector", lambda v, t=t: v.tensor_single_scalar(
                            mean[:, t:t + 1], sums[:, t:t + 1], 1.0 / D, ALU.mult),
                            deps=[ln2tk[("rs", t)]])
                    for t in C:
                        ln2tk[("s2", t)] = add("vector", lambda v, t=t: v.tensor_single_scalar(
                            var_[:, t:t + 1], sumsq[:, t:t + 1], 1.0 / D, ALU.mult),
                            deps=[ln2tk[("sq", t)]])
                    for t in B:
                        ln2tk[("rs", t)] = add("vector", lambda v, t=t: v.reduce_sum(
                            sums[:, t:t + 1], x_sb[:, t, :], axis=AX.X),
                            deps=[bo_add_tok[t]])
                    for t in B:
                        ln2tk[("sq", t)] = add("scalar", lambda ss, t=t: ss.activation(
                            sq2a[:, :], x_sb[:, t, :], AF.Square,
                            accum_out=sumsq[:, t:t + 1]), deps=[bo_add_tok[t]])
                    for t in C:
                        ln2tk[("mm", t)] = add("vector", lambda v, t=t: v.tensor_mul(
                            nmr[:, t:t + 1], mean[:, t:t + 1], mean[:, t:t + 1]),
                            deps=[ln2tk[("m", t)]])
                    for t in Dd:
                        ln2tk[("rst", t)] = add("vector", lambda v, t=t: v.reciprocal(
                            rstd[:, t:t + 1], rstd[:, t:t + 1]),
                            deps=[ln2tk[("std", t)]])
                    for t in C:
                        ln2tk[("var", t)] = add("vector", lambda v, t=t: v.tensor_sub(
                            var_[:, t:t + 1], var_[:, t:t + 1], nmr[:, t:t + 1]),
                            deps=[ln2tk[("s2", t)], ln2tk[("mm", t)]])
                    for t in C:
                        ln2tk[("std", t)] = add("scalar", lambda ss, t=t: ss.activation(
                            rstd[:, t:t + 1], var_[:, t:t + 1], AF.Sqrt,
                            bias=eps_sb[:, 0:1]), deps=[ln2tk[("var", t)], t_eps])
                    for t in Dd:
                        ln2tk[("nmr", t)] = add("vector", lambda v, t=t: v.scalar_tensor_tensor(
                            nmr[:, t:t + 1], mean[:, t:t + 1], -1.0, rstd[:, t:t + 1],
                            ALU.mult, ALU.mult), deps=[ln2tk[("rst", t)]])

                pend = [None]
                for pr in range(8):
                    hA, hB = PAIRS[pr]
                    g = hA % NGRP
                    wpar = pr % 2
                    wo_both = []
                    for half, h in ((0, hA), (1, hB)):
                        wo_both.append(add("gpsimd", lambda gg, h=h, half=half, wpar=wpar: gg.dma_start(
                            wo_sb[64 * half:64 * half + 64, wpar, :],
                            wo[64 * h:64 * h + 64, :]),
                            deps=[wo_free[wpar]] + (b_exit if pr < 2 else []),
                            dma=True))
                    wo_tok[wpar] = wo_both
                    for qb in range(NQB):
                        par = it % 2
                        exp_toks = {}
                        pv_mm = None
                        pv_this = {}

                        def plan_scores(s):
                            sp = s % 2
                            mm = {}
                            for u in range(2):
                                kt = 2 * s + u
                                dd0 = [sc_free.get(sp)] if u == 0 else []
                                if it == 0 and s < 2:
                                    dd0 = dd0 + b_exit
                                mm[0] = add("tensor", lambda tn, kt=kt, u=u, sp=sp, g=g, pr=pr, qb=qb: tn.matmul(
                                    sAB[sp][:, 256 * u:256 * u + 256],
                                    kT[0:64, g, 128 * kt:128 * kt + 128],
                                    qT[0:64, pr, QB * qb:QB * qb + QB],
                                    start=(u == 0), stop=(u == 1),
                                    tile_position=(0, 0)), deps=dd0)
                                mm[1] = add("tensor", lambda tn, kt=kt, u=u, sp=sp, g=g, pr=pr, qb=qb: tn.matmul(
                                    sAB[sp][:, 512 + 256 * u:512 + 256 * u + 256],
                                    kT[64:128, g, 128 * kt:128 * kt + 128],
                                    qT[64:128, pr, QB * qb:QB * qb + QB],
                                    start=(u == 0), stop=(u == 1),
                                    tile_position=(64, 0)),
                                    deps=([sc_free.get(sp)] if u == 0 else []))
                            dd = [mm[0], mm[1], bias_ptok[pr],
                                  pv_slot_prev.get((s, 0, 0)),
                                  pv_slot_prev.get((s, 0, 1)),
                                  pv_slot_prev.get((s, 1, 0)),
                                  pv_slot_prev.get((s, 1, 1))]
                            t_e = add("scalar", lambda sc_, sp=sp, pr=pr, s=s: sc_.activation(
                                p_sb[:, 1024 * s:1024 * s + 1024], sAB[sp][:, :],
                                AF.Exp, bias=bias_p[:, pr:pr + 1], scale=0.125),
                                deps=dd)
                            exp_toks[(s, 0)] = t_e
                            exp_toks[(s, 1)] = t_e
                            sc_free[sp] = t_e

                        def plan_pv(s):
                            nonlocal pv_mm
                            for half in range(2):
                                for u in range(2):
                                    kt = 2 * s + u
                                    st = (s == 0 and half == 0 and u == 0)
                                    sp_ = (s == 15 and half == 1 and u == 1)
                                    dd = [exp_toks[(s, half)]]
                                    if st:
                                        dd += [norm_done[par], t_vones]
                                    pv_mm = add("tensor", lambda tn, kt=kt, half=half, st=st, sp_=sp_, par=par, g=g, s=s: tn.matmul(
                                        pvb[par][0:65, 256 * half:256 * half + 256],
                                        v_sb[:, g, 65 * kt:65 * kt + 65],
                                        p_sb[:, 1024 * s + 512 * half + 256 * (kt % 2):1024 * s + 512 * half + 256 * (kt % 2) + 256],
                                        start=st, stop=sp_), deps=dd)
                                    pv_this[(s, half, u)] = pv_mm

                        for s in range(16):
                            plan_scores(s)
                            if s == 5 and pend[0] is not None:
                                cx0 = pend[0]
                                plan_norm(cx0)
                                pend[0] = None
                                if cx0["pr"] == 7:
                                    plan_x1_final(cx0["qb"])
                            if s >= 2:
                                plan_pv(s - 2)
                        plan_pv(14)
                        plan_pv(15)
                        pv_slot_prev = pv_this
                        pend[0] = {"par": par, "wpar": wpar, "qb": qb,
                                   "pv_mm": pv_mm, "pr": pr}
                        it += 1
                plan_norm(pend[0])
                for k2 in range(7, 11):
                    plan_x1_final(k2)
                pend[0] = None

                if DEBUG:
                    add("gpsimd", lambda g: g.dma_start(dbg_q[:, :, :], qT[:, :, :]),
                        deps=[bo_add_tok[NT - 1]], dma=True)
                    add("gpsimd", lambda g: g.dma_start(dbg_k[:, :, :], kT[:, :, :]),
                        dma=True)
                    add("gpsimd", lambda g: g.dma_start(dbg_v[:, :, :], v_sb[:, :, :]),
                        dma=True)
                    add("sync", lambda s: s.dma_start(dbg_bias[:, :], bias_h[:, :]),
                        deps=[bo_add_tok[NT - 1]], dma=True)
                    add("sync", lambda s: s.dma_start(dbg_x1[:, :, :], x_sb[:, :, :]),
                        deps=[bo_add_tok[t2] for t2 in range(NT)], dma=True)

            c_exit = P.barrier_tokens()

        CAP = 384
        NTT = 8
        NSC = CAP // 128
        BIG = 1.0e6
        with contextlib.ExitStack() as dctx:
            Ed = dctx.enter_context
            ln2N = Ed(nc.sbuf_tensor("ln2N", [128, NT, D], BF16))
            h_sb = Ed(nc.sbuf_tensor("h_sb", [128, 2, 4, CAP], BF16))
            w1_sb = Ed(nc.sbuf_tensor("w1_sb", [128, 2, 8, 256], BF16))
            w2_sb = Ed(nc.sbuf_tensor("w2_sb", [128, 2, 4, D], BF16))
            b1r = Ed(nc.sbuf_tensor("b1r", [1, 2, 256], BF16))
            sq2 = Ed(nc.sbuf_tensor("sq2", [128, D], BF16))
            lnTg = Ed(nc.sbuf_tensor("lnTg", [128, 2, 8, CAP], BF16))
            c_g = Ed(nc.sbuf_tensor("c_g", [1, 2, CAP], BF16))
            pmat = Ed(nc.sbuf_tensor("pmat", [128, 2, NTT, CAP], BF16))
            p0scr = Ed(nc.sbuf_tensor("p0scr", [128, 2, CAP], BF16))
            p0s = Ed(nc.sbuf_tensor("p0s", [128, 2, NSC, 1024], BF16))
            yacc = Ed(nc.sbuf_tensor("yacc", [128, 2, NSC, D], F32))
            rank_sb = Ed(nc.sbuf_tensor("rank_sb", [128, NT], F32))
            msk16 = Ed(nc.sbuf_tensor("msk16", [128, NT], BF16))
            umask = Ed(nc.sbuf_tensor("umask", [128, NT], F32))

            ones_c = Ed(nc.sbuf_tensor("ones_c", [128, 1], BF16))
            iotaS = nc.alloc_sbuf_tensor_at(
                "iotaS", [128, CAP], F32, offset=nc.lookup_mloc(sq2).addr)
            y16 = nc.alloc_sbuf_tensor_at(
                "y16", [128, 2, NSC, D], BF16, offset=nc.lookup_mloc(pmat).addr)
            lnT2 = nc.alloc_sbuf_tensor_at(
                "lnT2", [128, 2, 8, 128], BF16, offset=nc.lookup_mloc(h_sb).addr)
            ph = [Ed(nc.psum_tensor(f"ph{i}", [128, 512], F32)) for i in range(2)]
            py = [Ed(nc.psum_tensor(f"py{i}", [128, 512], F32)) for i in range(2)]
            ptr2 = [Ed(nc.psum_tensor(f"ptr2_{i}", [128, 128], BF16)) for i in range(2)]
            pg = Ed(nc.psum_tensor("pg", [128, 512], F32))
            pgt = Ed(nc.psum_tensor("pgt", [128, 512], F32))

            t_oc = add("vector", lambda v: v.memset(ones_c[:, :], 1.0))


            w1dma = {}
            w2dma = {}
            pair_done = {}

            def plan_w1_dma(m, half, dep):
                if m >= 8 * NEXP:
                    return
                e, mj = divmod(m, 8)
                j = 2 * mj + half
                t_a = add("gpsimd", lambda g, e=e, j=j, half=half: g.dma_start(
                    w1_sb[:, half, :, :],
                    w1[e, :, 256 * j:256 * j + 256].rearrange(
                        "(c p) f -> p c f", p=128)), deps=dep, dma=True)
                t_b = add(
                    "gpsimd", lambda g, e=e, j=j, half=half: g.dma_start(
                        b1r[0:1, half, :],
                        b1[e, 256 * j:256 * j + 256][None, :]), dma=True)
                w1dma[2 * m + half] = [t_a, t_b]

            def plan_w2_dma(m, dep):
                if m >= 8 * NEXP:
                    return
                e, mj = divmod(m, 8)
                w2dma[m] = add("gpsimd", lambda g, e=e, mj=mj, m=m: g.dma_start(
                    w2_sb[:, m % 2, :, :],
                    w2[e, 512 * mj:512 * mj + 512, :].rearrange(
                        "(c p) f -> p c f", p=128)), deps=dep, dma=True)

            plan_w1_dma(0, 0, c_exit)
            plan_w1_dma(0, 1, [])
            plan_w2_dma(0, [])
            plan_w2_dma(1, [])

            ptr2_free = [None, None]
            lnT2_free = [None, None]
            gbanks = None
            gfree4 = [[None], [None], [None], [None]]
            c_tok = {}
            cT_toks = []
            ln2N_toks = {}
            for g0 in range(0, NT, 4):
                batch = list(range(g0, g0 + 4))
                tk = {}
                for t in batch:
                    dd = [ln2tk[("nmr", t)]] + (c_exit if t == 0 else [])
                    tk[("ln", t)] = add("scalar", lambda ss, t=t: ss.activation(
                        ln2N[:, t, :], x_sb[:, t, :], AF.Identity,
                        bias=nmr[:, t:t + 1], scale=rstd[:, t:t + 1]),
                        deps=dd)
                    ln2N_toks[t] = tk[("ln", t)]
                for t in batch:
                    t_ln = tk[("ln", t)]
                    ev_toks = []
                    for c in range(8):
                        pp = c % 2
                        t_tr = add("tensor", lambda tn, c=c, pp=pp, t=t: tn.transpose(
                            ptr2[pp][:, :], ln2N[:, t, 128 * c:128 * c + 128],
                            idn[:, :]), deps=[t_ln, ptr2_free[pp], t_idn,
                                              lnT2_free[t % 2]])
                        t_ev = add("vector", lambda v, c=c, t=t, pp=pp: v.tensor_copy(
                            lnT2[:, t % 2, c, :], ptr2[pp][:, :]), deps=[t_tr])
                        ptr2_free[pp] = t_ev
                        ev_toks.append(t_ev)
                    gbk = [pg, pgt, py[0], py[1]][t % 4]
                    gfree = gfree4[t % 4]
                    gmm = None
                    for ci in range(8):
                        dd2 = [t_gw, gfree[0], ev_toks[ci]] if ci == 0 else [ev_toks[ci]]
                        gmm = add("tensor", lambda tn, ci=ci, t=t, gbk=gbk: tn.matmul(
                            gbk[:, 0:NEXP], lnT2[:, t % 2, ci, :],
                            gw_sb[:, ci, :],
                            start=(ci == 0), stop=False), deps=dd2)
                    gmm = add("tensor", lambda tn, gbk=gbk: tn.matmul(
                        gbk[:, 0:NEXP], ones[0:1, 0:128],
                        gb_sb[0:1, :], start=False, stop=True),
                        deps=[t_gb, t_ones])
                    lnT2_free[t % 2] = gmm
                    tk[("gmm", t)] = gmm
                q = {}
                for t in batch:
                    sl = t % 4
                    gbk = [pg, pgt, py[0], py[1]][t % 4]
                    q[("g1", t)] = add("vector", lambda v, sl=sl, gbk=gbk: v.reduce_max(
                        gsc[:, sl, 0:1], gbk[:, 0:NEXP], axis=AX.X),
                        deps=[tk[("gmm", t)]])
                for t in batch:
                    sl = t % 4
                    gbk = [pg, pgt, py[0], py[1]][t % 4]
                    q[("k1", t)] = add("vector", lambda v, sl=sl, gbk=gbk: v.tensor_single_scalar(
                        gmsk[:, sl, 0, :], gbk[:, 0:NEXP], gsc[:, sl, 0:1],
                        ALU.is_ge), deps=[q[("g1", t)]], aps=True)
                for t in batch:
                    sl = t % 4
                    gbk = [pg, pgt, py[0], py[1]][t % 4]
                    q[("ms", t)] = add("vector", lambda v, sl=sl, gbk=gbk: v.scalar_tensor_tensor(
                        gmsk[:, sl, 1, :], gmsk[:, sl, 0, :], -1e30,
                        gbk[:, 0:NEXP], ALU.mult, ALU.add), deps=[q[("k1", t)]])
                    gfree4[t % 4][0] = q[("ms", t)]
                for t in batch:
                    sl = t % 4
                    q[("g2", t)] = add("vector", lambda v, sl=sl: v.reduce_max(
                        gsc[:, sl, 1:2], gmsk[:, sl, 1, :], axis=AX.X),
                        deps=[q[("ms", t)]])
                for t in batch:
                    sl = t % 4
                    q[("k2", t)] = add("vector", lambda v, sl=sl: v.tensor_single_scalar(
                        gmsk[:, sl, 2, :], gmsk[:, sl, 1, :], gsc[:, sl, 1:2],
                        ALU.is_ge), deps=[q[("g2", t)]], aps=True)
                for t in batch:
                    sl = t % 4
                    q[("df", t)] = add("vector", lambda v, sl=sl: v.tensor_sub(
                        gsc[:, sl, 2:3], gsc[:, sl, 0:1], gsc[:, sl, 1:2]),
                        deps=[q[("g2", t)]])
                for t in batch:
                    sl = t % 4
                    q[("p1", t)] = add("scalar", lambda ss, sl=sl: ss.activation(
                        gsc[:, sl, 3:4], gsc[:, sl, 2:3], AF.Sigmoid),
                        deps=[q[("df", t)]])
                for t in batch:
                    sl = t % 4
                    q[("p2", t)] = add("vector", lambda v, sl=sl: v.tensor_scalar(
                        gsc[:, sl, 4:5], gsc[:, sl, 3:4], -1.0, 1.0,
                        ALU.mult, ALU.add), deps=[q[("p1", t)]])
                for t in batch:
                    sl = t % 4
                    q[("c1", t)] = add("vector", lambda v, t=t, sl=sl: v.tensor_scalar_mul(
                        c_sb[:, t, :], gmsk[:, sl, 0, :], gsc[:, sl, 3:4]),
                        deps=[q[("p2", t)]])
                for t in batch:
                    sl = t % 4
                    c_tok[t] = add("vector", lambda v, t=t, sl=sl: v.scalar_tensor_tensor(
                        c_sb[:, t, :], gmsk[:, sl, 2, :], gsc[:, sl, 4:5],
                        c_sb[:, t, :], ALU.mult, ALU.add),
                        deps=[q[("c1", t)], q[("k2", t)]], aps=True)

            pg_free = [gfree4[0][0]]
            pcT_free = [gfree4[0][0]]
            for t in range(NT):
                t_trc = add("tensor", lambda tn, t=t: tn.transpose(
                    pg[0:8, 0:128], c_sb[:, t, :], idn32[:, :]),
                    deps=[c_tok[t], pcT_free[0], t_idn32])
                t_evc = add("vector", lambda v, t=t: v.tensor_copy(
                    cT[:, 128 * t:128 * t + 128], pg[0:8, 0:128]), deps=[t_trc])
                pcT_free[0] = t_evc
                cT_toks.append(t_evc)
            pg_free[0] = pcT_free[0]

            if DEBUG:
                add("sync", lambda s: s.dma_start(dbg_c[:, :, :], c_sb[:, :, :]),
                    deps=[c_tok[t2] for t2 in range(NT)], dma=True)

            ltri = nc.alloc_sbuf_tensor_at(
                "ltri", [128, 128], BF16, offset=nc.lookup_mloc(sq2).addr + 1536)
            allon = nc.alloc_sbuf_tensor_at(
                "allon", [128, 128], BF16, offset=nc.lookup_mloc(sq2).addr + 1792)
            t_is = add("gpsimd", lambda g: g.iota(
                iotaS[:, :], [[1, CAP]], base=0, channel_multiplier=0,
                allow_small_or_imprecise_dtypes=True),
                deps=[P.last["scalar"]])
            t_lt = add("vector", lambda v: v.tensor_single_scalar(
                ltri[:, :], iot[:, :], 0.0, ALU.is_gt),
                deps=[t_iota, P.last["scalar"]])
            t_ao = add("vector", lambda v: v.memset(allon[:, :], 1.0),
                       deps=[P.last["scalar"]])

            gather_done = [None]
            scatter_done = [None]
            h_free = None
            ph_free = [None, None]
            pyp = [py[0], py[1], pg, pgt]
            py_free = [gfree4[2][0], gfree4[3][0], pg_free[0], gfree4[1][0]]
            pgt_free = [None]
            x_wtok = {}
            yacc_tok = {}
            lnTg_tok = {}
            cg_tok = {}
            prev_y16r = [None]
            rl_tok = {}
            out_toks = []

            for e in range(NEXP):
                t_mk = add("vector", lambda v, e=e: v.tensor_single_scalar(
                    msk16[:, :], c_sb[:, :, e], 0.0, ALU.is_gt),
                    deps=[c_tok[NT - 1]])
                t_um = add("vector", lambda v: v.tensor_scalar(
                    umask[:, :], msk16[:, :], -1.0, 1.0, ALU.mult, ALU.add),
                    deps=[t_mk])
                rmm = None
                first = True
                for th in range(2):
                    for tt in range(NTT):
                        t_abs = 8 * th + tt
                        last = (th == 1 and tt == NTT - 1)
                        for j in range(tt + 1):
                            lhs = ltri if j == tt else allon
                            rmm = add("tensor", lambda tn, lhs=lhs, t_abs=t_abs, th=th, j=j, first=first, stop=(last and j == tt): tn.matmul(
                                pg[:, t_abs:t_abs + 1], lhs[:, :],
                                msk16[:, 8 * th + j:8 * th + j + 1],
                                start=first, stop=stop),
                                deps=([t_mk, t_lt, t_ao, py_free[2]] if first else []))
                            first = False
                t_rk = add("vector", lambda v: v.scalar_tensor_tensor(
                    rank_sb[:, :], umask[:, :], BIG, pg[:, 0:NT],
                    ALU.mult, ALU.add), deps=[rmm, t_um])
                py_free[2] = t_rk

                pm_toks = []
                pmat_last = [None]
                for th in range(2):
                    for tt in range(NTT):
                        t_abs = 8 * th + tt
                        t_pm = add("vector", lambda v, th=th, tt=tt, t_abs=t_abs, e=e: v.tensor_scalar(
                            pmat[:, th, tt, :], iotaS[:, :],
                            rank_sb[:, t_abs:t_abs + 1],
                            c_sb[:, t_abs, e:e + 1],
                            ALU.is_equal, ALU.mult),
                            deps=[t_rk, t_is, gather_done[0], prev_y16r[0]],
                            aps=True)
                        pm_toks.append(t_pm)
                        pmat_last[0] = t_pm
                        pp0 = t_abs % 2
                        t_p0 = add("vector", lambda v, t_abs=t_abs, pp0=pp0: v.tensor_single_scalar(
                            p0scr[:, pp0, :], iotaS[:, :],
                            rank_sb[:, t_abs:t_abs + 1], ALU.is_equal),
                            deps=[t_rk, t_is], aps=True)
                        for scc in range(NSC):
                            pp = scc % 2
                            t_tp = add("tensor", lambda tn, pp0=pp0, scc=scc, pp=pp: tn.transpose(
                                ptr2[pp][:, :],
                                p0scr[:, pp0, 128 * scc:128 * scc + 128],
                                idn[:, :]),
                                deps=[t_p0, ptr2_free[pp], scatter_done[0]])
                            t_pe = add("vector", lambda v, th=th, scc=scc, tt=tt, pp=pp: v.tensor_copy(
                                p0s[:, th, scc, 128 * tt:128 * tt + 128],
                                ptr2[pp][:, :]), deps=[t_tp])
                            ptr2_free[pp] = t_pe
                            pm_toks.append(t_pe)

                gmm_last = None
                gidx = 0
                gfree = [py_free[3], py_free[2]]
                for th in range(2):
                    for dc in range(9):
                        gbk = [pgt, pg][gidx % 2]
                        gmm = None
                        for tt in range(NTT):
                            if dc < 8:
                                lh = (lambda tn, th=th, tt=tt, dc=dc: ln2N[:, 8 * th + tt, 128 * dc:128 * dc + 128])
                            else:
                                lh = (lambda tn: ones_c[:, 0:1])
                            dd3 = ([pmat_last[0], gfree[gidx % 2], t_oc,
                                    ln2N_toks[8 * th + tt]] if tt == 0 else [])
                            gmm = add("tensor", lambda tn, lh=lh, th=th, tt=tt, dc=dc, gbk=gbk: tn.matmul(
                                gbk[0:(128 if dc < 8 else 1), 0:CAP], lh(tn),
                                pmat[:, th, tt, :],
                                start=(tt == 0), stop=(tt == NTT - 1)), deps=dd3)
                        if dc < 8:
                            t_ge = add("vector", lambda v, th=th, dc=dc, gbk=gbk: v.tensor_copy(
                                lnTg[:, th, dc, :], gbk[:, 0:CAP]),
                                deps=[gmm, h_free, lnTg_tok.get((th, dc))])
                            lnTg_tok[(th, dc)] = t_ge
                        else:
                            t_ge = add("vector", lambda v, th=th, gbk=gbk: v.tensor_copy(
                                c_g[0:1, th, :], gbk[0:1, 0:CAP]),
                                deps=[gmm, h_free, cg_tok.get(th)])
                            cg_tok[th] = t_ge
                        gfree[gidx % 2] = t_ge
                        gidx += 1
                        gmm_last = gmm
                py_free[2], py_free[3] = gfree[1], gfree[0]
                gather_done[0] = gmm_last

                for mj in range(8):
                    m = 8 * e + mj
                    for half in range(2):
                        for th in range(2):
                            for fc in range(2):
                                pb = (2 * th + fc) % 2
                                hmm = None
                                for ci in range(8):
                                    dd4 = []
                                    if ci == 0:
                                        dd4 = [w1dma[2 * m + half], ph_free[pb],
                                               lnTg_tok[(th, 0)], lnTg_tok[(th, 7)]]
                                    hmm = add("tensor", lambda tn, th=th, fc=fc, ci=ci, half=half, pb=pb: tn.matmul(
                                        ph[pb][:, 0:CAP],
                                        w1_sb[:, half, ci, 128 * fc:128 * fc + 128],
                                        lnTg[:, th, ci, :],
                                        start=(ci == 0), stop=False), deps=dd4)
                                hmm = add("tensor", lambda tn, th=th, fc=fc, half=half, pb=pb: tn.matmul(
                                    ph[pb][:, 0:CAP],
                                    b1r[0:1, half, 128 * fc:128 * fc + 128],
                                    c_g[0:1, th, :],
                                    start=False, stop=True),
                                    deps=[cg_tok[th]])
                                slot = 2 * half + fc
                                t_rl = add("scalar", lambda s, th=th, slot=slot, pb=pb: s.activation(
                                    h_sb[:, th, slot, :],
                                    ph[pb][:, 0:CAP], AF.Relu),
                                    deps=[hmm, rl_tok.get((th, slot))])
                                ph_free[pb] = t_rl
                                h_free = t_rl
                                rl_tok[(th, slot)] = t_rl
                        plan_w1_dma(m + 1, half, [P.last["tensor"]])
                    yq = 0
                    for th in range(2):
                        for st in range(NSC):
                            for dh in range(2):
                                pp = yq % 4
                                yq += 1
                                ymm = None
                                for cj in range(4):
                                    dd5 = [py_free[pp], rl_tok[(th, 3)],
                                           rl_tok[(th, 2)]] if cj == 0 else []
                                    ymm = add("tensor", lambda tn, th=th, st=st, dh=dh, cj=cj, pp=pp, m=m: tn.matmul(
                                        pyp[pp][:, :],
                                        h_sb[:, th, cj, 128 * st:128 * st + 128],
                                        w2_sb[:, m % 2, cj, 512 * dh:512 * dh + 512],
                                        start=(cj == 0), stop=(cj == 3)),
                                        deps=dd5 + ([w2dma[m]] if cj == 0 else []))
                                if mj == 0:
                                    t_ya = add("vector", lambda v, th=th, st=st, dh=dh, pp=pp: v.tensor_copy(
                                        yacc[:, th, st, 512 * dh:512 * dh + 512],
                                        pyp[pp][:, :]),
                                        deps=[ymm, yacc_tok.get((th, st, dh))])
                                else:
                                    t_ya = add("vector", lambda v, th=th, st=st, dh=dh, pp=pp: v.tensor_add(
                                        yacc[:, th, st, 512 * dh:512 * dh + 512],
                                        yacc[:, th, st, 512 * dh:512 * dh + 512],
                                        pyp[pp][:, :]),
                                        deps=[ymm, yacc_tok.get((th, st, dh))])
                                yacc_tok[(th, st, dh)] = t_ya
                                py_free[pp] = t_ya
                    pair_done[m] = P.last["tensor"]
                    plan_w2_dma(m + 2, [pair_done[m]])

                yc_toks = []
                yc_th = {}
                for th in range(2):
                    for st in range(NSC):
                        for dh in range(2):
                            t_yc = add("vector", lambda v, th=th, st=st, dh=dh: v.tensor_copy(
                                y16[:, th, st, 512 * dh:512 * dh + 512],
                                yacc[:, th, st, 512 * dh:512 * dh + 512]),
                                deps=[yacc_tok[(th, st, dh)], gather_done[0]])
                            yc_toks.append(t_yc)
                    yc_th[th] = yc_toks[-1]
                smm_last = None
                sq_i = 0
                for th in range(2):
                    for tt in range(NTT):
                        t_abs = 8 * th + tt
                        for dh in range(2):
                            pp = sq_i % 4
                            sq_i += 1
                            smm = None
                            for scc in range(NSC):
                                dd6 = [py_free[pp], yc_th[th], pm_toks[-1]] if scc == 0 else []
                                smm = add("tensor", lambda tn, th=th, tt=tt, dh=dh, scc=scc, pp=pp: tn.matmul(
                                    pyp[pp][:, :],
                                    p0s[:, th, scc, 128 * tt:128 * tt + 128],
                                    y16[:, th, scc, 512 * dh:512 * dh + 512],
                                    start=(scc == 0), stop=(scc == NSC - 1)),
                                    deps=dd6)
                            smm_last = smm
                            t_xa = add("vector", lambda v, t_abs=t_abs, dh=dh, pp=pp: v.tensor_add(
                                x_sb[:, t_abs, 512 * dh:512 * dh + 512],
                                x_sb[:, t_abs, 512 * dh:512 * dh + 512],
                                pyp[pp][:, :]),
                                deps=[smm, x_wtok.get((t_abs, dh))])
                            x_wtok[(t_abs, dh)] = t_xa
                            py_free[pp] = t_xa
                        if e == NEXP - 1:
                            for dh in range(2):
                                pp = sq_i % 4
                                sq_i += 1
                                t_bm = add("tensor", lambda tn, t_abs=t_abs, dh=dh, pp=pp: tn.matmul(
                                    pyp[pp][:, :], cT[0:8, 128 * t_abs:128 * t_abs + 128],
                                    b2_sb[0:8, 512 * dh:512 * dh + 512],
                                    start=True, stop=True),
                                    deps=[cT_toks[t_abs], t_b2l, py_free[pp]])
                                t_fa = add("vector", lambda v, t_abs=t_abs, dh=dh, pp=pp: v.tensor_add(
                                    x_sb[:, t_abs, 512 * dh:512 * dh + 512],
                                    x_sb[:, t_abs, 512 * dh:512 * dh + 512],
                                    pyp[pp][:, :]), deps=[t_bm, x_wtok[(t_abs, dh)]])
                                py_free[pp] = t_fa
                                x_wtok[(t_abs, dh)] = t_fa
                            oeng = ("sync", "gpsimd")[t_abs % 2]
                            out_toks.append(add(oeng, lambda en, t_abs=t_abs: en.dma_start(
                                out[128 * t_abs:128 * t_abs + 128, :],
                                x_sb[:, t_abs, :]),
                                deps=[x_wtok[(t_abs, 0)], x_wtok[(t_abs, 1)]],
                                dma=True))
                scatter_done[0] = smm_last
                prev_y16r[0] = smm_last

            add("sync", lambda s: s.nop(), deps=out_toks)

        LAST_PLAN["n_ops"] = len(P.ops)
        LAST_PLAN["P"] = P
        P.prepare()
        with nc.Block() as block:

            @block.tensor
            def _(e):
                P.emit_engine("tensor", e, prog_sem, dma_sem)

            @block.vector
            def _(e):
                P.emit_engine("vector", e, prog_sem, dma_sem)

            @block.scalar
            def _(e):
                P.emit_engine("scalar", e, prog_sem, dma_sem)

            @block.gpsimd
            def _(e):
                P.emit_engine("gpsimd", e, prog_sem, dma_sem)

            @block.sync
            def _(e):
                P.emit_engine("sync", e, prog_sem, dma_sem)
    return nc


_CACHE = {}


def kernel(**inputs):
    if "nc" not in _CACHE:
        _CACHE["nc"] = build_program()
    nc = _CACHE["nc"]
    f32 = lambda k: np.ascontiguousarray(np.asarray(inputs[k], dtype=np.float32))
    x = f32("L0_attn_x")
    W = {
        "wq": f32("L0_attn_Wq"), "bq": f32("L0_attn_bq"),
        "wk": f32("L0_attn_Wk"), "bk": f32("L0_attn_bk"),
        "wv": f32("L0_attn_Wv"), "bv": f32("L0_attn_bv"),
        "wo": f32("L0_attn_Wo"), "bo": f32("L0_attn_bo"),
        "gw": f32("L0_ffn_gate_w"), "gb": f32("L0_ffn_gate_b"),
        "w1": f32("L0_ffn_w1"), "b1": f32("L0_ffn_b1"),
        "w2": f32("L0_ffn_w2"), "b2": f32("L0_ffn_b2"),
    }
    in_maps = []
    for c in range(8):
        b, h = divmod(c, 2)
        m = dict(W)
        m["x_own"] = np.ascontiguousarray(x[b, 2048 * h:2048 * h + 2048])
        m["x_oth"] = np.ascontiguousarray(x[b, 2048 * (1 - h):2048 * (1 - h) + 2048])
        in_maps.append(m)
    res = run_bass_kernel_spmd(nc, in_maps, core_ids=list(range(8)))
    outv = np.empty((4, 4096, 1024), np.float32)
    for c in range(8):
        b, h = divmod(c, 2)
        outv[b, 2048 * h:2048 * h + 2048] = res.results[c]["out"]
    return outv
